# version 08 (final)
# baseline (speedup 1.0000x reference)
import numpy as np
import ml_dtypes
from contextlib import ExitStack

import concourse.bass as bass
import concourse.mybir as mybir
from concourse.bass_utils import run_bass_kernel_spmd

F32 = mybir.dt.float32
BF16 = mybir.dt.bfloat16
AF = mybir.ActivationFunctionType
ALU = mybir.AluOpType
AX = mybir.AxisListType

P = 128
H = 2048
S = 2048
NB = 2
NCORES = 8
NT = S // P
HC = H // P
NHEAD = 16
NKV = 4
DHEAD = 128
QKV = 3072
I_SH = 5632
I_MOE = 1408
E = 8
MASK_NEG = -2.3819763e38
EPS = 1e-5
SCALE = DHEAD ** -0.5

TB = 1024
NTB = (NB * S) // TB
TPB = TB // P
CPB = I_MOE // P
NIB = 4 + E

DEBUG = False
DEBUG2 = False
PHASES = "full"
FFN_BLOCKS = NTB
WSLOTS = 2

SEMS = ["ld", "lda", "ldb", "st", "so", "w", "w2", "wd", "pe", "dve", "act", "ht", "dn"]


class Prog:
    def __init__(self, nc, stack, marks=None):
        self.nc = nc
        self.final = marks is not None
        self.marks = dict(marks) if marks else {}
        self.sems = {}
        self.cnt = {}
        self.waited = {}
        self.missing = 0
        for s in SEMS:
            self.sems[s] = stack.enter_context(nc.semaphore(s))
            self.cnt[s] = 0

    def inc(self, instr, sem, k=1, label=None):
        instr.then_inc(self.sems[sem], k)
        self.cnt[sem] += k
        if label is not None:
            self._set(label, (sem, self.cnt[sem]))
        return (sem, self.cnt[sem])

    def _set(self, label, val):
        if not self.final:
            self.marks[label] = val
        else:
            assert self.marks.get(label) == val, (
                f"mark drift {label}: {self.marks.get(label)} vs {val}")

    def mark_here(self, sem, label):
        self._set(label, (sem, self.cnt[sem]))

    def wait(self, eng, engname, mark):
        if isinstance(mark, str):
            if mark not in self.marks:
                assert not self.final, f"unknown mark {mark}"
                self.missing += 1
                return
            mark = self.marks[mark]
        sem, v = mark
        if v <= 0:
            return
        key = (engname, sem)
        if self.waited.get(key, -1) >= v:
            return
        eng.wait_ge(self.sems[sem], v)
        self.waited[key] = v

    def fence_record(self, label):
        self._set("F_" + label, tuple((s, self.cnt[s]) for s in SEMS))

    def fence_wait(self, eng, engname, label):
        key = "F_" + label
        if key not in self.marks:
            assert not self.final
            self.missing += 1
            return
        for sem, v in self.marks[key]:
            self.wait(eng, engname, (sem, v))


def build_program(marks=None):
    nc = bass.Bass("TRN2", target_bir_lowering=False)

    x_in = nc.dram_tensor("x", [NB * S, H], F32, kind="ExternalInput").ap()
    w_qkv = nc.dram_tensor("w_qkv", [H, QKV], F32, kind="ExternalInput").ap()
    w_o = nc.dram_tensor("w_o", [H, H], F32, kind="ExternalInput").ap()
    w_r = nc.dram_tensor("w_r", [E, H], F32, kind="ExternalInput").ap()
    w_sg = nc.dram_tensor("w_sg", [1, H], F32, kind="ExternalInput").ap()
    w_sgu = nc.dram_tensor("w_sgu", [H, 2 * I_SH], F32, kind="ExternalInput").ap()
    w_sd = nc.dram_tensor("w_sd", [I_SH, H], F32, kind="ExternalInput").ap()
    w_egu = nc.dram_tensor("w_egu", [E * H, 2 * I_MOE], F32, kind="ExternalInput").ap()
    w_ed = nc.dram_tensor("w_ed", [E * I_MOE, H], F32, kind="ExternalInput").ap()
    c_eyef = nc.dram_tensor("c_eyef", [P, P], F32, kind="ExternalInput").ap()
    c_eyeb = nc.dram_tensor("c_eyeb", [P, P], BF16, kind="ExternalInput").ap()
    c_zerob = nc.dram_tensor("c_zerob", [P, P], BF16, kind="ExternalInput").ap()
    c_mask = nc.dram_tensor("c_mask", [P, P], F32, kind="ExternalInput").ap()
    c_rope = nc.dram_tensor("c_rope", [S, 256], BF16, kind="ExternalInput").ap()

    out = nc.dram_tensor("out", [NB * S, H], F32, kind="ExternalOutput").ap()

    qt_scr = nc.dram_tensor("qt_scr", [NB * NT * HC * P, P], BF16, kind="Internal").ap()
    at_scr = nc.dram_tensor("at_scr", [NB * 4 * HC * P, 512], BF16, kind="Internal").ap()
    l2_scr = nc.dram_tensor("l2_scr", [NB * NT * HC * P, P], BF16, kind="Internal").ap()

    dbg = {}
    if DEBUG:
        dbg["kT"] = nc.dram_tensor("dbg_kT", [P, NKV * S], BF16, kind="ExternalOutput").ap()
        dbg["v"] = nc.dram_tensor("dbg_v", [P, NT * 512], BF16, kind="ExternalOutput").ap()
        dbg["qt"] = nc.dram_tensor("dbg_qt", [NB * NT * HC * P, P], BF16, kind="ExternalOutput").ap()
        dbg["at"] = nc.dram_tensor("dbg_at", [NB * 4 * HC * P, 512], BF16, kind="ExternalOutput").ap()
        dbg["l2"] = nc.dram_tensor("dbg_l2", [NB * NT * HC * P, P], BF16, kind="ExternalOutput").ap()
        dbg["comb"] = nc.dram_tensor("dbg_comb", [P, NB * NT * E], F32, kind="ExternalOutput").ap()
        dbg["sg"] = nc.dram_tensor("dbg_sg", [P, NB * NT], F32, kind="ExternalOutput").ap()
    if DEBUG2:
        dbg["stat"] = nc.dram_tensor("dbg_stat", [P, 16], F32, kind="ExternalOutput").ap()
        dbg["lnx"] = nc.dram_tensor("dbg_lnx", [P, H], BF16, kind="ExternalOutput").ap()
        dbg["xt"] = nc.dram_tensor("dbg_xt", [P, H], F32, kind="ExternalOutput").ap()
        dbg["p8"] = nc.dram_tensor("dbg_p8", [P, 4 * E], F32, kind="ExternalOutput").ap()
        dbg["st2"] = nc.dram_tensor("dbg_st2", [P, 16], F32, kind="ExternalOutput").ap()
        dbg["flnx"] = nc.dram_tensor("dbg_flnx", [P, HC * TB], BF16, kind="ExternalOutput").ap()
        dbg["fht"] = nc.dram_tensor("dbg_fht", [P, CPB * TB], BF16, kind="ExternalOutput").ap()
        dbg["fwg"] = nc.dram_tensor("dbg_fwg", [P, HC * P], BF16, kind="ExternalOutput").ap()
        dbg["fht1"] = nc.dram_tensor("dbg_fht1", [P, TB], BF16, kind="ExternalOutput").ap()
        dbg["fsilu"] = nc.dram_tensor("dbg_fsilu", [P, 1024], F32, kind="ExternalOutput").ap()
        dbg["fwu"] = nc.dram_tensor("dbg_fwu", [P, HC * P], BF16, kind="ExternalOutput").ap()

    stack = ExitStack()
    with stack:
        pr = Prog(nc, stack, marks)

        def sb(st, name, shape, dt):
            return st.enter_context(nc.sbuf_tensor(name, shape, dt))

        def psum(st, name, shape, dt=F32):
            return st.enter_context(nc.psum_tensor(name, shape, dt))

        eyef = sb(stack, "eyef", [P, P], F32)
        eyeb = sb(stack, "eyeb", [P, P], BF16)
        zerob = sb(stack, "zerob", [P, P], BF16)
        maskd = sb(stack, "maskd", [P, P], F32)
        ropec = sb(stack, "ropec", [P, NT * 256], BF16)
        wrsT = sb(stack, "wrsT", [P, HC * 9], F32)
        sig_buf = sb(stack, "sig_buf", [P, NB * NT], F32)
        comb_buf = sb(stack, "comb_buf", [P, NB * NT * E], F32)
        stat = sb(stack, "stat", [P, 16], F32)
        p8 = sb(stack, "p8", [P, 4 * E], F32)

        def stc(i):
            return stat[:, i:i + 1]

        env = dict(x_in=x_in, w_qkv=w_qkv, w_o=w_o, w_sgu=w_sgu, w_sd=w_sd,
                   c_zerob_d=c_zerob, c_mask_d=c_mask,
                   w_egu=w_egu, w_ed=w_ed, out=out, qt_scr=qt_scr,
                   at_scr=at_scr, l2_scr=l2_scr, eyef=eyef, eyeb=eyeb,
                   zerob=zerob, maskd=maskd, ropec=ropec, wrsT=wrsT,
                   sig_buf=sig_buf, comb_buf=comb_buf, stat=stat, stc=stc,
                   p8=p8, dbg=dbg)

        block = stack.enter_context(nc.Block())

        @block.sync
        def _(sp):
            pr.inc(sp.dma_start(out=eyef[:, :], in_=c_eyef[:, :]), "ld", 16)
            pr.inc(sp.dma_start(out=eyeb[:, :], in_=c_eyeb[:, :]), "ld", 16)
            pr.inc(sp.dma_start(out=zerob[:, :], in_=c_zerob[:, :]), "ld", 16)
            pr.inc(sp.dma_start(out=maskd[:, :], in_=c_mask[:, :]), "ld", 16)
            i = sp.dma_start(out=ropec.rearrange("p (t f) -> p t f", f=256),
                             in_=c_rope.rearrange("(t p) f -> p t f", p=P))
            pr.inc(i, "ld", 16, "c_all")

        with ExitStack() as st0:
            wr_raw = sb(st0, "wr_raw", [P, H], F32)
            ps0 = psum(st0, "ps0", [P, HC * 9], F32)

            @block.sync
            def _(sp):
                pr.inc(sp.dma_start(out=wr_raw[0:E, :], in_=w_r[:, :]), "ld", 16)
                i = sp.dma_start(out=wr_raw[E:E + 1, :], in_=w_sg[:, :])
                pr.inc(i, "ld", 16, "wr_raw")

            @block.tensor
            def _(t):
                pr.wait(t, "t", "wr_raw")
                pr.wait(t, "t", "c_all")
                for c in range(HC):
                    i = t.transpose(ps0[:, c * 9:(c + 1) * 9],
                                    wr_raw[0:9, c * P:(c + 1) * P],
                                    eyef[0:9, 0:9])
                pr.inc(i, "pe", 1, "wrsT_ps")

            @block.vector
            def _(v):
                v.memset(stc(13), EPS)
                v.memset(stc(15), 1.0)
                v.memset(comb_buf[:, :], 0.0)
                v.memset(sig_buf[:, :], 0.0)
                pr.wait(v, "v", "wrsT_ps")
                i = v.tensor_copy(wrsT[:, :], ps0[:, :])
                pr.inc(i, "dve", 1, "wrsT_ok")

            pr.fence_record("p0")

        nseq = NB if PHASES == "full" else 1
        for s in range(nseq):
            with ExitStack() as stA:
                kT_sb = sb(stA, f"kT{s}", [P, NKV * S], BF16)
                v_sb = sb(stA, f"v{s}", [P, NT * 512], BF16)
                emit_A1(nc, pr, block, s, sb, psum, env, kT_sb, v_sb)
                if PHASES in ("a2", "a3", "full"):
                    emit_A2(nc, pr, block, s, sb, psum, env, kT_sb, v_sb)
            if PHASES in ("a3", "full"):
                emit_A3(nc, pr, block, s, sb, psum, env)

        if PHASES == "full":
            emit_FFN(nc, pr, block, sb, psum, env)

        if DEBUG:
            @block.sync
            def _(sp):
                if PHASES == "full":
                    pr.fence_wait(sp, "sp", "ffn")
                else:
                    pr.fence_wait(sp, "sp",
                                  {"a1": "a1_0", "a2": "a2_0", "a3": "a3_0"}[PHASES])
                if PHASES in ("a3", "full"):
                    pr.inc(sp.dma_start(out=dbg["comb"][:, :], in_=comb_buf[:, :]), "st", 16)
                    pr.inc(sp.dma_start(out=dbg["sg"][:, :], in_=sig_buf[:, :]), "st", 16)
                    pr.inc(sp.dma_start(out=dbg["l2"][:, :], in_=l2_scr[:, :]), "st", 16)
                if PHASES in ("a2", "a3", "full"):
                    pr.inc(sp.dma_start(out=dbg["at"][:, :], in_=at_scr[:, :]), "st", 16)
                i = sp.dma_start(out=dbg["qt"][:, :], in_=qt_scr[:, :])
                pr.inc(i, "st", 16, "dbg_done")

        @block.sync
        def _(sp):
            sp.wait_ge(pr.sems["st"], pr.cnt["st"])

    return nc, pr


def emit_A1(nc, pr, block, s, sb, psum, env, kT_sb, v_sb):
    x_in = env["x_in"]; w_qkv = env["w_qkv"]; qt_scr = env["qt_scr"]
    c_zerob_g = env["c_zerob_d"]
    eyeb = env["eyeb"]; ropec = env["ropec"]; stc = env["stc"]
    prev_fence = "p0" if s == 0 else f"a3_{s-1}"

    with ExitStack() as st1:
        wqkv_sb = sb(st1, f"wqkv{s}", [P, HC * QKV], BF16)
        x_tile = [sb(st1, f"x{s}_{i}", [P, H], F32) for i in range(2)]
        sq_scr = sb(st1, f"sq{s}", [P, H], F32)
        lnx_bf = sb(st1, f"lnx{s}", [P, H], BF16)
        lnxT_bf = sb(st1, f"lnxT{s}", [P, HC * P], BF16)
        q_tok = sb(st1, f"qtok{s}", [P, NHEAD * DHEAD], BF16)
        k_tok = sb(st1, f"ktok{s}", [P, NKV * DHEAD], BF16)
        ropet = sb(st1, f"ropet{s}", [P, 2048], F32)
        qT_tile = sb(st1, f"qTt{s}", [P, HC * P], BF16)
        ps_qkv = psum(st1, f"ps_qkv{s}", [P, QKV], F32)
        ps_tr = psum(st1, f"ps_tr{s}", [P, H], BF16)

        @block.gpsimd
        def _(g):
            pr.fence_wait(g, "g", prev_fence)
            i = g.dma_start(out=wqkv_sb.rearrange("p (c m) -> p c m", m=QKV),
                            in_=w_qkv.rearrange("(c p) m -> p c m", p=P))
            pr.inc(i, "wd", 16)
            i = g.dma_start(out=qT_tile[:, 0:P], in_=env["c_mask_d"][:, :])
            pr.inc(i, "wd", 16, f"a1w_{s}")

        @block.sync
        def _(sp):
            pr.fence_wait(sp, "sp", prev_fence)
            for t in range(NT):
                if t >= 2:
                    pr.wait(sp, "sp", f"a1ln_{s}_{t-2}")
                    pr.wait(sp, "sp", f"a1st_{s}_{t-2}")
                i = sp.dma_start(
                    out=x_tile[t % 2][:, :],
                    in_=x_in[(s * NT + t) * P:(s * NT + t + 1) * P, :])
                pr.inc(i, "lda" if t % 2 == 0 else "ldb", 16, f"a1x_{s}_{t}")

        @block.vector
        def _(v):
            pr.fence_wait(v, "v", prev_fence)
            for t in range(NT):
                pr.wait(v, "v", f"a1x_{s}_{t}")
                if DEBUG2 and s == 0 and t == 1:
                    pr.wait(v, "v", "dbg2_st")
                xt = x_tile[t % 2]
                v.reduce_sum(stc(0), xt[:, :], axis=AX.X)
                i = v.scalar_tensor_tensor(sq_scr[:, :], xt[:, :], 1.0,
                                           xt[:, :], ALU.bypass, ALU.mult,
                                           accum_out=stc(2))
                pr.inc(i, "dve", 1, f"a1st_{s}_{t}")
                pr.wait(v, "v", f"a1mn_{s}_{t}")
                i = v.scalar_tensor_tensor(stc(3), stc(2), 1.0 / H, stc(5),
                                           ALU.mult, ALU.add)
                pr.inc(i, "dve", 1, f"a1var_{s}_{t}")
                pr.wait(v, "v", f"a1rstd_{s}_{t}")
                i = v.tensor_scalar(stc(7), stc(1), stc(6), -1.0,
                                    ALU.mult, ALU.mult)
                pr.inc(i, "dve", 1, f"a1bias_{s}_{t}")

                pr.wait(v, "v", f"a1trl_{s}_{t}")
                i = v.tensor_copy(lnxT_bf[:, :], ps_tr[:, 0:HC * P])
                pr.inc(i, "dve", 1, f"a1lnxT_{s}_{t}")
                pr.wait(v, "v", f"a1qkv_{s}_{t}")
                if t >= 1:
                    pr.wait(v, "v", f"a1trq_{s}_{t-1}")
                    pr.wait(v, "v", f"a1trk_{s}_{t-1}")
                cq = ropec[:, t * 256 + 0:t * 256 + 64]
                sq = ropec[:, t * 256 + 64:t * 256 + 128]
                ck = ropec[:, t * 256 + 128:t * 256 + 192]
                sk = ropec[:, t * 256 + 192:t * 256 + 256]

                def rope_one(off, nh, cos, sin, dst):
                    xv = ps_qkv[:, off:off + nh * DHEAD].rearrange(
                        "p (h d) -> p h d", d=DHEAD)
                    x1 = xv[:, :, 0:64]
                    x2 = xv[:, :, 64:128]
                    cb = cos[:, None, :].broadcast_to([P, nh, 64])
                    sbb = sin[:, None, :].broadcast_to([P, nh, 64])
                    ta = ropet[:, 0:nh * 64].rearrange("p (h d) -> p h d", d=64)
                    tb = ropet[:, 1024:1024 + nh * 64].rearrange(
                        "p (h d) -> p h d", d=64)
                    dv = dst.rearrange("p (h d) -> p h d", d=DHEAD)
                    v.tensor_tensor(ta, x1, cb, ALU.mult)
                    v.tensor_tensor(tb, x2, sbb, ALU.mult)
                    v.tensor_sub(dv[:, :, 0:64], ta, tb)
                    v.tensor_tensor(ta, x1, sbb, ALU.mult)
                    v.tensor_tensor(tb, x2, cb, ALU.mult)
                    return v.tensor_add(dv[:, :, 64:128], ta, tb)

                i = rope_one(0, NHEAD, cq, sq, q_tok[:, :])
                pr.inc(i, "dve", 1, f"a1ropeq_{s}_{t}")
                i = rope_one(NHEAD * DHEAD, NKV, ck, sk, k_tok[:, :])
                pr.inc(i, "dve", 1, f"a1ropek_{s}_{t}")

                pr.wait(v, "v", f"a1trq_{s}_{t}")
                if t >= 1:
                    pr.wait(v, "v", f"a1qsp_{s}_{t-1}")
                i = v.tensor_copy(qT_tile[:, :], ps_tr[:, 0:HC * P])
                pr.inc(i, "dve", 1, f"a1qev_{s}_{t}")
                pr.wait(v, "v", f"a1trk_{s}_{t}")
                i = v.tensor_copy(
                    kT_sb.rearrange("p (c n) -> p c n", n=S)[:, :, t * P:(t + 1) * P],
                    ps_tr[:, 0:NKV * P].rearrange("p (c m) -> p c m", m=P))
                pr.inc(i, "dve", 1, f"a1kev_{s}_{t}")

        if DEBUG2 and s == 0:
            @block.gpsimd
            def _(gg):
                pr.wait(gg, "g", "a1rstd_0_0")
                pr.inc(gg.dma_start(out=env["dbg"]["stat"][:, :],
                                    in_=env["stat"][:, :]), "st", 16)
                pr.wait(gg, "g", "a1ln_0_0")
                pr.inc(gg.dma_start(out=env["dbg"]["lnx"][:, :],
                                    in_=lnx_bf[:, :]), "st", 16)
                i = gg.dma_start(out=env["dbg"]["xt"][:, :], in_=x_tile[0][:, :])
                pr.inc(i, "st", 16, "dbg2_st")

        @block.scalar
        def _(a):
            pr.fence_wait(a, "a", prev_fence)
            for t in range(NT):
                if DEBUG2 and s == 0 and t == 1:
                    pr.wait(a, "a", "dbg2_st")
                pr.wait(a, "a", f"a1st_{s}_{t}")
                a.activation(stc(1), stc(0), AF.Identity, scale=1.0 / H)
                a.activation(stc(4), stc(1), AF.Square)
                i = a.activation(stc(5), stc(4), AF.Identity, scale=-1.0)
                pr.inc(i, "act", 1, f"a1mn_{s}_{t}")
                pr.wait(a, "a", f"a1var_{s}_{t}")
                a.activation(stc(4), stc(3), AF.Ln, bias=stc(13))
                i = a.activation(stc(6), stc(4), AF.Exp, scale=-0.5)
                pr.inc(i, "act", 1, f"a1rstd_{s}_{t}")
                pr.wait(a, "a", f"a1bias_{s}_{t}")
                if t >= 1:
                    pr.wait(a, "a", f"a1trl_{s}_{t-1}")
                i = a.activation(lnx_bf[:, :], x_tile[t % 2][:, :], AF.Identity,
                                 bias=stc(7), scale=stc(6))
                pr.inc(i, "act", 1, f"a1ln_{s}_{t}")
                pr.wait(a, "a", f"a1qkv_{s}_{t}")
                i = a.copy(v_sb[:, t * 512:(t + 1) * 512],
                           ps_qkv[:, QKV - 512:QKV])
                pr.inc(i, "act", 1, f"a1v_{s}_{t}")
                pr.wait(a, "a", f"a1qev_{s}_{t}")
                i = a.dma_start(
                    out=qt_scr[(s * NT + t) * HC * P:(s * NT + t + 1) * HC * P, :]
                    .rearrange("(c p) m -> p c m", p=P),
                    in_=qT_tile.rearrange("p (c m) -> p c m", m=P))
                pr.inc(i, "st", 16, f"a1qsp_{s}_{t}")

        @block.tensor
        def _(te):
            pr.fence_wait(te, "t", prev_fence)
            pr.wait(te, "t", f"a1w_{s}")
            for t in range(NT):
                pr.wait(te, "t", f"a1ln_{s}_{t}")
                if t >= 1:
                    pr.wait(te, "t", f"a1kev_{s}_{t-1}")
                for c in range(HC):
                    i = te.transpose(ps_tr[:, c * P:(c + 1) * P],
                                     lnx_bf[:, c * P:(c + 1) * P], eyeb[:, :])
                pr.inc(i, "pe", 1, f"a1trl_{s}_{t}")
                pr.wait(te, "t", f"a1lnxT_{s}_{t}")
                if t >= 1:
                    pr.wait(te, "t", f"a1ropek_{s}_{t-1}")
                    pr.wait(te, "t", f"a1v_{s}_{t-1}")
                for n in range(QKV // 512):
                    for c in range(HC):
                        i = te.matmul(
                            ps_qkv[:, n * 512:(n + 1) * 512],
                            lnxT_bf[:, c * P:(c + 1) * P],
                            wqkv_sb[:, c * QKV + n * 512:c * QKV + (n + 1) * 512],
                            start=(c == 0), stop=(c == HC - 1))
                pr.inc(i, "pe", 1, f"a1qkv_{s}_{t}")
                pr.wait(te, "t", f"a1ropeq_{s}_{t}")
                for c in range(HC):
                    i = te.transpose(ps_tr[:, c * P:(c + 1) * P],
                                     q_tok[:, c * P:(c + 1) * P], eyeb[:, :])
                pr.inc(i, "pe", 1, f"a1trq_{s}_{t}")
                pr.wait(te, "t", f"a1ropek_{s}_{t}")
                pr.wait(te, "t", f"a1qev_{s}_{t}")
                for c in range(NKV):
                    i = te.transpose(ps_tr[:, c * P:(c + 1) * P],
                                     k_tok[:, c * P:(c + 1) * P], eyeb[:, :])
                pr.inc(i, "pe", 1, f"a1trk_{s}_{t}")

        if DEBUG and s == 0:
            @block.sync
            def _(sp):
                pr.wait(sp, "sp", f"a1kev_{s}_{NT-1}")
                pr.wait(sp, "sp", f"a1v_{s}_{NT-1}")
                pr.inc(sp.dma_start(out=env["dbg"]["kT"][:, :], in_=kT_sb[:, :]), "st", 16)
                i = sp.dma_start(out=env["dbg"]["v"][:, :], in_=v_sb[:, :])
                pr.inc(i, "st", 16, "dbg_a1")

        pr.fence_record(f"a1_{s}")


def emit_A2(nc, pr, block, s, sb, psum, env, kT_sb, v_sb):
    qt_scr = env["qt_scr"]; at_scr = env["at_scr"]
    eyef = env["eyef"]; zerob = env["zerob"]; maskd = env["maskd"]
    stc = env["stc"]
    NG = 4
    NSLOT = NG * NHEAD * 4

    def slots():
        g = 0
        for qg in range(NG):
            for h in range(NHEAD):
                for j in range(4):
                    yield qg, h, j, g
                    g += 1

    def hslot(qg, h):
        return qg * NHEAD + h

    def slot_of(g):
        qg, r = divmod(g, NHEAD * 4)
        h, j = divmod(r, 4)
        return qg, h, j

    def cmx(g):
        return stc(8) if g % 2 == 0 else stc(12)

    def crc(g):
        return stc(10) if g % 2 == 0 else stc(14)

    with ExitStack() as st2:
        qTg = sb(st2, f"qTg{s}", [P, HC * 512], BF16)
        scs = [sb(st2, f"scs{s}_{i}", [P, S], BF16) for i in range(2)]
        probs_sb = [sb(st2, f"probs{s}_{i}", [P, S], BF16) for i in range(2)]
        probsT = [sb(st2, f"pT{s}_{i}", [P, NT * 512], BF16) for i in range(2)]
        attnTg = sb(st2, f"aTg{s}", [P, HC * 512], BF16)
        dmat = [sb(st2, f"dm{s}_{i}", [P, P], BF16) for i in range(2)]
        ps_sc = psum(st2, f"ps_sc{s}", [P, S], F32)
        ps_pt = [psum(st2, f"ps_pt{s}_{i}", [P, 512], F32) for i in range(2)]
        ps_pv = [psum(st2, f"ps_pv{s}_{i}", [P, 512], F32) for i in range(2)]

        @block.sync
        def _(sp):
            pr.fence_wait(sp, "sp", f"a1_{s}")
            for qg in range(NG):
                if qg >= 1:
                    pr.wait(sp, "sp", f"a2qk_{s}_{qg * NHEAD * 4 - 1}")
                for jj in range(4):
                    gq = (s * NT + qg * 4 + jj) * HC * P
                    i = sp.dma_start(
                        out=qTg.rearrange("p (c n) -> p c n", n=512)[
                            :, :, jj * P:(jj + 1) * P],
                        in_=qt_scr[gq:gq + HC * P, :].rearrange(
                            "(c p) m -> p c m", p=P))
                    pr.inc(i, "ld", 16, f"a2q_{s}_{qg}" if jj == 3 else None)

        @block.tensor
        def _(t):
            pr.fence_wait(t, "t", f"a1_{s}")
            ptb = 0

            def emit_dmm(g):
                nonlocal ptb
                qg, h, j = slot_of(g)
                qi = qg * 4 + j
                kmax = (qi + 1) * P
                nkc = 4 * (qg + 1)
                pr.wait(t, "t", f"a2dm_{s}_{g}")
                pr.wait(t, "t", f"a2exp_{s}_{g}")
                nb4 = (nkc + 3) // 4
                for b4 in range(nb4):
                    if ptb >= 2:
                        pr.wait(t, "t", f"a2ptev_{s}_{ptb-2}")
                        pass
                    pb = ps_pt[ptb % 2]
                    n4 = min(4, nkc - b4 * 4)
                    for k4 in range(n4):
                        kc = b4 * 4 + k4
                        live = kc * P < kmax
                        i = t.matmul(
                            pb[:, k4 * P:(k4 + 1) * P],
                            probs_sb[g % 2][:, kc * P:(kc + 1) * P] if live
                            else zerob[:, :],
                            dmat[j % 2][:, :] if live else zerob[:, :],
                            start=True, stop=True)
                    pr.inc(i, "pe", 1, f"a2pt_{s}_{ptb}")
                    ptb += 1
                pr.mark_here("pe", f"a2ptmm_{s}_{g}")
                if j == 3:
                    hs = hslot(qg, h)
                    kv = h // 4
                    pr.wait(t, "t", f"a2ptall_v_{s}_{hs}")
                    pr.wait(t, "t", f"a2ptall_a_{s}_{hs}")
                    if hs >= 2:
                        pr.wait(t, "t", f"a2pvev_{s}_{hs-2}")
                    for kc in range(nkc):
                        i = t.matmul(
                            ps_pv[h % 2][:, :],
                            v_sb[:, kc * 512 + kv * P:kc * 512 + (kv + 1) * P],
                            probsT[h % 2][:, kc * 512:(kc + 1) * 512],
                            start=(kc == 0), stop=(kc == nkc - 1))
                    pr.inc(i, "pe", 1, f"a2pv_{s}_{hs}")

            for qg, h, j, g in slots():
                kv = h // 4
                qi = qg * 4 + j
                kmax = (qi + 1) * P
                if j == 0 and h == 0:
                    pr.wait(t, "t", f"a2q_{s}_{qg}")
                if g >= 1:
                    pr.wait(t, "t", f"a2sev_{s}_{g-1}")
                lhs = qTg[:, h * 512 + j * P:h * 512 + (j + 1) * P]
                nfull, rem = kmax // 512, kmax % 512
                for c in range(nfull):
                    i = t.matmul(ps_sc[:, c * 512:(c + 1) * 512], lhs,
                                 kT_sb[:, kv * S + c * 512:kv * S + (c + 1) * 512],
                                 start=True, stop=True)
                if rem:
                    i = t.matmul(ps_sc[:, nfull * 512:nfull * 512 + rem], lhs,
                                 kT_sb[:, kv * S + nfull * 512:kv * S + nfull * 512 + rem],
                                 start=True, stop=True)
                pr.inc(i, "pe", 1, f"a2qk_{s}_{g}")
                if g >= 2:
                    emit_dmm(g - 2)
            emit_dmm(NSLOT - 2)
            emit_dmm(NSLOT - 1)

        def evac_batches(g, engine, engname, want_parity, sem):
            qg2, h2, j2 = slot_of(g)
            nkc2 = 4 * (qg2 + 1)
            nb42 = (nkc2 + 3) // 4
            base_ptb = sum((4 * (slot_of(x)[0] + 1) + 3) // 4 for x in range(g))
            for b4 in range(nb42):
                ptb = base_ptb + b4
                if ptb % 2 == want_parity:
                    pr.wait(engine, engname, f"a2pt_{s}_{ptb}")
                    n4 = min(4, nkc2 - b4 * 4)
                    if engname == "a":
                        i = engine.copy(
                            probsT[h2 % 2].rearrange("p (c n) -> p c n", n=512)[
                                :, b4 * 4:b4 * 4 + n4, j2 * P:(j2 + 1) * P],
                            ps_pt[ptb % 2].rearrange("p (c n) -> p c n", n=P)[
                                :, 0:n4, :])
                        pr.inc(i, sem, 1, f"a2ptev_{s}_{ptb}")
                    else:
                        i = engine.tensor_copy(
                            probsT[h2 % 2].rearrange("p (c n) -> p c n", n=512)[
                                :, b4 * 4:b4 * 4 + n4, j2 * P:(j2 + 1) * P],
                            ps_pt[ptb % 2].rearrange("p (c n) -> p c n", n=P)[
                                :, 0:n4, :])
                        pr.inc(i, sem, 1, f"a2ptev_{s}_{ptb}")
            if j2 == 3:
                pr.mark_here(sem, f"a2ptall_{engname}_{s}_{hslot(qg2, h2)}")

        @block.scalar
        def _(a):
            pr.fence_wait(a, "a", f"a1_{s}")

            def emit_sev(g):
                qg2, h2, j2 = slot_of(g)
                km = (qg2 * 4 + j2 + 1) * P
                pr.wait(a, "a", f"a2qk_{s}_{g}")
                i = a.activation(scs[g % 2][:, 0:km], ps_sc[:, 0:km],
                                 AF.Identity)
                pr.inc(i, "act", 1, f"a2sev_{s}_{g}")

            def act_evac(gx):
                evac_batches(gx, a, "a", 1, "act")

            def act_pvev(gx):
                qg2, h2, j2 = slot_of(gx)
                if j2 != 3:
                    return
                hs2 = hslot(qg2, h2)
                pr.wait(a, "a", f"a2pv_{s}_{hs2}")
                if qg2 >= 1:
                    pr.wait(a, "a", f"a2asp_{s}_{qg2-1}")
                i2 = a.copy(attnTg[:, h2 * 512:(h2 + 1) * 512],
                            ps_pv[h2 % 2][:, :])
                pr.inc(i2, "act", 1, f"a2pvev_{s}_{hs2}")
                if h2 == NHEAD - 1:
                    i2 = a.dma_start(
                        out=at_scr[(s * 4 + qg2) * HC * P:(s * 4 + qg2 + 1) * HC * P, :]
                        .rearrange("(c p) n -> p c n", p=P),
                        in_=attnTg.rearrange("p (c n) -> p c n", n=512))
                    pr.inc(i2, "st", 16, f"a2asp_{s}_{qg2}")

            emit_sev(0)
            for qg, h, j, g in slots():
                qi = qg * 4 + j
                kmax = (qi + 1) * P
                if g >= 2:
                    act_evac(g - 2)
                if g >= 3:
                    act_pvev(g - 3)
                if g + 1 < NSLOT:
                    emit_sev(g + 1)
                pr.wait(a, "a", f"a2mx_{s}_{g}")
                if g >= 2:
                    pr.wait(a, "a", f"a2ptmm_{s}_{g-2}")
                i = a.activation(probs_sb[g % 2][:, 0:kmax],
                                 scs[g % 2][:, 0:kmax],
                                 AF.Exp, bias=cmx(g), accum_out=stc(9))
                pr.inc(i, "act", 1, f"a2exp_{s}_{g}")
                a.activation(stc(11), stc(9), AF.Ln)
                if g >= 2:
                    pr.wait(a, "a", f"a2dm_{s}_{g-2}")
                i = a.activation(crc(g), stc(11), AF.Exp, scale=-1.0)
                pr.inc(i, "act", 1, f"a2rcp_{s}_{g}")
                if g == NSLOT - 1:
                    act_evac(NSLOT - 2)
                    act_evac(NSLOT - 1)
                    act_pvev(NSLOT - 3)
                    act_pvev(NSLOT - 2)
                    act_pvev(NSLOT - 1)

        @block.vector
        def _(v):
            pr.fence_wait(v, "v", f"a1_{s}")

            def emit_dm(g):
                qg2, h2, j2 = slot_of(g)
                pr.wait(v, "v", f"a2rcp_{s}_{g}")
                if g >= 2:
                    pr.wait(v, "v", f"a2ptmm_{s}_{g-2}")
                i = v.tensor_scalar(dmat[j2 % 2][:, :], eyef[:, :], crc(g),
                                    None, ALU.mult)
                pr.inc(i, "dve", 1, f"a2dm_{s}_{g}")
                hs = hslot(qg2, h2)
                if j2 == 0 and hs >= 2:
                    pr.wait(v, "v", f"a2pv_{s}_{hs-2}")

            for qg, h, j, g in slots():
                qi = qg * 4 + j
                kmax = (qi + 1) * P
                pr.wait(v, "v", f"a2sev_{s}_{g}")
                sc = scs[g % 2]
                v.tensor_add(sc[:, kmax - P:kmax], sc[:, kmax - P:kmax],
                             maskd[:, :])
                if g >= 2:
                    pr.wait(v, "v", f"a2exp_{s}_{g-2}")
                i = v.reduce_max(cmx(g), sc[:, 0:kmax], axis=AX.X, negate=True)
                pr.inc(i, "dve", 1, f"a2mx_{s}_{g}")
                if g >= 1:
                    emit_dm(g - 1)
                if g >= 2:
                    evac_batches(g - 2, v, "v", 0, "dve")
            emit_dm(NSLOT - 1)
            evac_batches(NSLOT - 2, v, "v", 0, "dve")
            evac_batches(NSLOT - 1, v, "v", 0, "dve")

        pr.fence_record(f"a2_{s}")


def g3(s, t):
    return s * NT + t


def emit_A3(nc, pr, block, s, sb, psum, env):
    x_in = env["x_in"]; w_o = env["w_o"]; out = env["out"]
    at_scr = env["at_scr"]; l2_scr = env["l2_scr"]
    wrsT = env["wrsT"]; eyef = env["eyef"]
    sig_buf = env["sig_buf"]; comb_buf = env["comb_buf"]
    stc = env["stc"]; p8 = env["p8"]

    with ExitStack() as st3:
        wo_sb = sb(st3, f"wo{s}", [P, HC * H], BF16)
        atTg = sb(st3, f"atTg{s}", [P, HC * 512], BF16)
        x_tile = [sb(st3, f"x3_{s}_{i}", [P, H], F32) for i in range(2)]
        x2_sb = sb(st3, f"x2_{s}", [P, H], F32)
        sq3 = sb(st3, f"sq3_{s}", [P, H], F32)
        lnx2_f = sb(st3, f"l2f_{s}", [P, H], F32)
        lnx2T_f = sb(st3, f"l2Tf_{s}", [P, HC * P], F32)
        lnx2T_b = sb(st3, f"l2Tb_{s}", [P, HC * P], BF16)
        ps_o = psum(st3, f"ps_o{s}", [P, H], F32)
        ps_t2 = psum(st3, f"ps_t2{s}", [P, 1024], F32)
        ps_r = psum(st3, f"ps_r{s}", [P, 9], F32)

        @block.gpsimd
        def _(g):
            pr.fence_wait(g, "g", f"a2_{s}")
            pr.inc(g.dma_start(out=wo_sb.rearrange("p (c m) -> p c m", m=H),
                               in_=w_o.rearrange("(c p) m -> p c m", p=P)),
                   "wd", 16)
            i = g.dma_start(out=lnx2T_b[:, 0:P], in_=env["c_mask_d"][:, :])
            pr.inc(i, "wd", 16, f"a3w_{s}")
            if DEBUG2 and s == 0:
                pr.wait(g, "g", "a3cmb_0_0")
                pr.wait(g, "g", "a3sg_0_0")
                pr.inc(g.dma_start(out=env["dbg"]["p8"][:, :],
                                   in_=p8[:, :]), "st", 16)
                i = g.dma_start(out=env["dbg"]["st2"][:, :],
                                in_=env["stat"][:, :])
                pr.inc(i, "st", 16, "dbg3_st")

        @block.sync
        def _(sp):
            pr.fence_wait(sp, "sp", f"a2_{s}")
            for qg in range(4):
                if qg >= 1:
                    pr.wait(sp, "sp", f"a3wo_{s}_{qg-1}_3")
                i = sp.dma_start(
                    out=atTg.rearrange("p (c n) -> p c n", n=512),
                    in_=at_scr[(s * 4 + qg) * HC * P:(s * 4 + qg + 1) * HC * P, :]
                    .rearrange("(c p) n -> p c n", p=P))
                pr.inc(i, "ld", 16, f"a3at_{s}_{qg}")
                for j in range(4):
                    t = qg * 4 + j
                    if t >= 2:
                        pr.wait(sp, "sp", f"a3x2_{s}_{t-2}")
                    i = sp.dma_start(
                        out=x_tile[t % 2][:, :],
                        in_=x_in[(s * NT + t) * P:(s * NT + t + 1) * P, :])
                    pr.inc(i, "lda" if t % 2 == 0 else "ldb", 16, f"a3x_{s}_{t}")

        @block.tensor
        def _(te):
            pr.fence_wait(te, "t", f"a2_{s}")
            pr.wait(te, "t", f"a3w_{s}")

            def emit_tr_rt(t):
                for half in range(2):
                    pr.wait(te, "t", f"a3l2_{s}_{t}")
                    if t >= 1 or half >= 1:
                        pr.wait(te, "t", f"a3tev_{s}_{t}_{half-1}"
                                if half >= 1 else f"a3tev_{s}_{t-1}_1")
                    for c in range(8):
                        i = te.transpose(
                            ps_t2[:, c * P:(c + 1) * P],
                            lnx2_f[:, (half * 8 + c) * P:(half * 8 + c + 1) * P],
                            eyef[:, :])
                    pr.inc(i, "pe", 1, f"a3tr_{s}_{t}_{half}")
                pr.wait(te, "t", f"a3tev_{s}_{t}_1")
                if t >= 1:
                    pr.wait(te, "t", f"a3r8_{s}_{t-1}")
                for c in range(HC):
                    i = te.matmul(ps_r[:, :],
                                  lnx2T_f[:, c * P:(c + 1) * P],
                                  wrsT[:, c * 9:(c + 1) * 9],
                                  start=(c == 0), stop=(c == HC - 1))
                pr.inc(i, "pe", 1, f"a3rt_{s}_{t}")

            for qg in range(4):
                pr.wait(te, "t", f"a3at_{s}_{qg}")
                for j in range(4):
                    t = qg * 4 + j
                    if t >= 1:
                        pr.wait(te, "t", f"a3x2_{s}_{t-1}")
                    for c in range(HC):
                        for n in range(4):
                            i = te.matmul(
                                ps_o[:, n * 512:(n + 1) * 512],
                                atTg[:, c * 512 + j * P:c * 512 + (j + 1) * P],
                                wo_sb[:, c * H + n * 512:c * H + (n + 1) * 512],
                                start=(c == 0), stop=(c == HC - 1))
                    pr.inc(i, "pe", 1, f"a3wo_{s}_{qg}_{j}")
                    if t >= 1:
                        emit_tr_rt(t - 1)
            emit_tr_rt(NT - 1)

        @block.vector
        def _(v):
            pr.fence_wait(v, "v", f"a2_{s}")
            for t in range(NT):
                pr.wait(v, "v", f"a3wo_{s}_{t//4}_{t%4}")
                pr.wait(v, "v", f"a3x_{s}_{t}")
                if t >= 1:
                    pr.wait(v, "v", f"a3l2_{s}_{t-1}")
                    pr.wait(v, "v", f"a3xsp_{s}_{t-1}")
                i = v.tensor_add(x2_sb[:, :], ps_o[:, :], x_tile[t % 2][:, :])
                pr.inc(i, "dve", 1, f"a3x2_{s}_{t}")
                v.reduce_sum(stc(0), x2_sb[:, :], axis=AX.X)
                i = v.scalar_tensor_tensor(sq3[:, :], x2_sb[:, :], 1.0,
                                           x2_sb[:, :], ALU.bypass, ALU.mult,
                                           accum_out=stc(2))
                pr.inc(i, "dve", 1, f"a3st_{s}_{t}")
                pr.wait(v, "v", f"a3mn_{s}_{t}")
                i = v.scalar_tensor_tensor(stc(3), stc(2), 1.0 / H, stc(5),
                                           ALU.mult, ALU.add)
                pr.inc(i, "dve", 1, f"a3var_{s}_{t}")
                pr.wait(v, "v", f"a3rstd_{s}_{t}")
                i = v.tensor_scalar(stc(7), stc(1), stc(6), -1.0,
                                    ALU.mult, ALU.mult)
                pr.inc(i, "dve", 1, f"a3bias_{s}_{t}")
                for half in range(2):
                    pr.wait(v, "v", f"a3tr_{s}_{t}_{half}")
                    v.tensor_copy(lnx2T_f[:, half * 1024:(half + 1) * 1024],
                                  ps_t2[:, :])
                    i = v.tensor_copy(lnx2T_b[:, half * 1024:(half + 1) * 1024],
                                      ps_t2[:, :])
                    pr.inc(i, "dve", 1, f"a3tev_{s}_{t}_{half}")
                g = s * NT + t
                pr.wait(v, "v", f"a3rt_{s}_{t}")
                i = v.reduce_max(stc(8), ps_r[:, 0:E], axis=AX.X, negate=True)
                pr.inc(i, "dve", 1, f"a3rmx_{s}_{t}")
                pr.wait(v, "v", f"a3rcp_{s}_{t}")
                i = v.tensor_scalar(p8[:, 0:E], p8[:, 2 * E:3 * E], stc(10),
                                    None, ALU.mult)
                pr.inc(i, "dve", 1, f"a3rnm_{s}_{t}")
                i = v.max(out=p8[:, E:2 * E], in_=p8[:, 2 * E:3 * E])
                pr.inc(i, "dve", 1, f"a3srt_{s}_{t}")
                pr.wait(v, "v", f"a3pb_{s}_{t}")
                i = v.scalar_tensor_tensor(
                    comb_buf[:, g * E:(g + 1) * E], p8[:, 2 * E:3 * E],
                    stc(14), p8[:, 3 * E:4 * E], ALU.is_ge, ALU.mult)
                pr.inc(i, "dve", 1, f"a3cmb_{s}_{t}")

        @block.scalar
        def _(a):
            pr.fence_wait(a, "a", f"a2_{s}")
            for t in range(NT):
                pr.wait(a, "a", f"a3st_{s}_{t}")
                a.activation(stc(1), stc(0), AF.Identity, scale=1.0 / H)
                a.activation(stc(4), stc(1), AF.Square)
                i = a.activation(stc(5), stc(4), AF.Identity, scale=-1.0)
                pr.inc(i, "act", 1, f"a3mn_{s}_{t}")
                pr.wait(a, "a", f"a3var_{s}_{t}")
                a.activation(stc(4), stc(3), AF.Ln, bias=stc(13))
                i = a.activation(stc(6), stc(4), AF.Exp, scale=-0.5)
                pr.inc(i, "act", 1, f"a3rstd_{s}_{t}")
                pr.wait(a, "a", f"a3bias_{s}_{t}")
                if t >= 1:
                    pr.wait(a, "a", f"a3tr_{s}_{t-1}_1")
                i = a.activation(lnx2_f[:, :], x2_sb[:, :], AF.Identity,
                                 bias=stc(7), scale=stc(6))
                pr.inc(i, "act", 1, f"a3l2_{s}_{t}")
                pr.wait(a, "a", f"a3rmx_{s}_{t}")
                if DEBUG2 and s == 0 and t == 1:
                    pr.wait(a, "a", "dbg3_st")
                if t >= 1:
                    pr.wait(a, "a", f"a3cmb_{s}_{t-1}")
                i = a.activation(p8[:, 2 * E:3 * E], ps_r[:, 0:E], AF.Exp,
                                 bias=stc(8), accum_out=stc(9))
                pr.inc(i, "act", 1, f"a3rex_{s}_{t}")
                a.activation(stc(11), stc(9), AF.Ln)
                i = a.activation(stc(10), stc(11), AF.Exp, scale=-1.0)
                pr.inc(i, "act", 1, f"a3rcp_{s}_{t}")
                a.activation(stc(12), ps_r[:, E:E + 1], AF.Exp, scale=-1.0)
                a.activation(stc(11), stc(12), AF.Ln, bias=stc(15))
                i = a.activation(sig_buf[:, g3(s, t):g3(s, t) + 1], stc(11),
                                 AF.Exp, scale=-1.0)
                pr.inc(i, "act", 1, f"a3sg_{s}_{t}")
                pr.mark_here("act", f"a3r8_{s}_{t}")
                pr.wait(a, "a", f"a3srt_{s}_{t}")
                pr.wait(a, "a", f"a3rnm_{s}_{t}")
                a.activation(stc(14), p8[:, E + 1:E + 2], AF.Identity)
                i = a.activation(p8[:, 3 * E:4 * E], p8[:, 0:E], AF.Identity)
                pr.inc(i, "act", 1, f"a3pb_{s}_{t}")
                pr.wait(a, "a", f"a3x2_{s}_{t}")
                i = a.dma_start(out=out[(s * NT + t) * P:(s * NT + t + 1) * P, :],
                                in_=x2_sb[:, :])
                pr.inc(i, "so", 16, f"a3xsp_{s}_{t}")
                pr.wait(a, "a", f"a3tev_{s}_{t}_1")
                i = a.dma_start(
                    out=l2_scr[(s * NT + t) * HC * P:(s * NT + t + 1) * HC * P, :]
                    .rearrange("(c p) m -> p c m", p=P),
                    in_=lnx2T_b.rearrange("p (c m) -> p c m", m=P))
                pr.inc(i, "st", 16, f"a3lsp_{s}_{t}")

        pr.fence_record(f"a3_{s}")


def emit_FFN(nc, pr, block, sb, psum, env):
    c_zerob_g = env["c_zerob_d"]
    c_mask_g = env["c_mask_d"]
    out = env["out"]; l2_scr = env["l2_scr"]
    sig_buf = env["sig_buf"]; comb_buf = env["comb_buf"]
    w_sgu = env["w_sgu"]; w_sd = env["w_sd"]
    w_egu = env["w_egu"]; w_ed = env["w_ed"]

    with ExitStack() as st4:
        lnx2T = sb(st4, "f_lnx2T", [P, HC * TB], BF16)
        hT = sb(st4, "f_hT", [P, CPB * TB], BF16)
        wg = [sb(st4, f"f_wg{i}", [P, HC * P], BF16) for i in range(2)]
        wu = [sb(st4, f"f_wu{i}", [P, HC * P], BF16) for i in range(2)]
        dn = sb(st4, "f_dn", [P, CPB * H], BF16)
        outb = sb(st4, "f_out", [P, TPB * H], F32)
        x2t = sb(st4, "f_x2", [P, H], F32)
        silu_g = sb(st4, "f_silu", [P, 1024], F32)
        wfence = sb(st4, "f_wfence", [P, P], BF16)
        ps_g = [psum(st4, f"f_psg{i}", [P, 512], F32) for i in range(2)]
        ps_u = [psum(st4, f"f_psu{i}", [P, 512], F32) for i in range(2)]
        ps_d = [psum(st4, f"f_psd{i}", [P, 1024], F32) for i in range(2)]

        def gu_slices(ib, c):
            if ib < 4:
                i0 = ib * I_MOE + c * P
                return (w_sgu[:, i0:i0 + P], w_sgu[:, I_SH + i0:I_SH + i0 + P])
            e = ib - 4
            i0 = c * P
            return (w_egu[e * H:(e + 1) * H, i0:i0 + P],
                    w_egu[e * H:(e + 1) * H, I_MOE + i0:I_MOE + i0 + P])

        def dn_slice(ib, c):
            if ib < 4:
                r0 = ib * I_MOE + c * P
                return w_sd[r0:r0 + P, :]
            e = ib - 4
            r0 = e * I_MOE + c * P
            return w_ed[r0:r0 + P, :]

        @block.sync
        def _(sp):
            pr.fence_wait(sp, "sp", "a3_1")
            for tb in range(FFN_BLOCKS):
                if tb >= 1:
                    for j in range(TPB):
                        gtp = (tb - 1) * TPB + j
                        pr.wait(sp, "sp", f"fevL_{tb-1}_{j}")
                        i = sp.dma_start(out=out[gtp * P:(gtp + 1) * P, :],
                                         in_=outb[:, j * H:(j + 1) * H])
                        pr.inc(i, "so", 16, f"fst_{tb-1}_{j}")
                    pr.wait(sp, "sp", f"fgu_last_{tb-1}")
                for jj in range(TPB):
                    gl = (tb * TPB + jj) * HC * P
                    i = sp.dma_start(
                        out=lnx2T.rearrange("p (c n) -> p c n", n=TB)[
                            :, :, jj * P:(jj + 1) * P],
                        in_=l2_scr[gl:gl + HC * P, :].rearrange(
                            "(c p) m -> p c m", p=P))
                    pr.inc(i, "ld", 16, f"fln_{tb}" if jj == TPB - 1 else None)
                for j in range(TPB):
                    gt = tb * TPB + j
                    if gt >= 1:
                        pr.wait(sp, "sp", f"fev0_{gt-1}")
                    i = sp.dma_start(out=x2t[:, :],
                                     in_=out[gt * P:(gt + 1) * P, :])
                    pr.inc(i, "lda", 16, f"fx2_{gt}")

        @block.gpsimd
        def _(g):
            pr.fence_wait(g, "g", "a3_1")
            ci = 0
            for tb in range(FFN_BLOCKS):
                for ib in range(NIB):
                    for c in range(CPB):
                        slot = ci % WSLOTS
                        if ci >= WSLOTS:
                            pr.wait(g, "g", f"fguh_{2*(ci-WSLOTS)+1}")
                        gate, up = gu_slices(ib, c)
                        wsem = "w" if slot == 0 else "w2"
                        if DEBUG2 and ci == 3:
                            pr.wait(g, "g", "dbg5_st")
                        pr.inc(g.dma_start(
                            out=wg[slot].rearrange("p (c m) -> p c m", m=P),
                            in_=gate.rearrange("(c p) m -> p c m", p=P)), wsem, 16)
                        pr.inc(g.dma_start(
                            out=wu[slot].rearrange("p (c m) -> p c m", m=P),
                            in_=up.rearrange("(c p) m -> p c m", p=P)), wsem, 16)
                        i = g.dma_start(out=wfence[:, :], in_=c_mask_g[:, :])
                        pr.inc(i, wsem, 16, f"fw_{ci}")
                        ci += 1
                    if tb + ib > 0:
                        pr.wait(g, "g", f"fdn_done_{tb}_{ib-1}" if ib > 0
                                else f"fdn_done_{tb-1}_{NIB-1}")
                    for c in range(CPB):
                        pr.inc(g.dma_start(out=dn[:, c * H:(c + 1) * H],
                                           in_=dn_slice(ib, c)), "wd", 16)
                    i = g.dma_start(out=wfence[:, :], in_=c_mask_g[:, :])
                    pr.inc(i, "wd", 16, f"fdnw_{tb}_{ib}")

        @block.tensor
        def _(te):
            pr.fence_wait(te, "t", "a3_1")
            ci = 0
            gup = 0
            dnp = 0
            for tb in range(FFN_BLOCKS):
                pr.wait(te, "t", f"fln_{tb}")
                for ib in range(NIB):
                    for c in range(CPB):
                        pr.wait(te, "t", f"fw_{ci}")
                        slot = ci % WSLOTS
                        if tb + ib > 0:
                            pr.wait(te, "t", f"fdn_done_{tb}_{ib-1}" if ib > 0
                                    else f"fdn_done_{tb-1}_{NIB-1}")
                        for th in range(2):
                            if gup >= 2:
                                pr.wait(te, "t", f"fhev_{gup-2}")
                            pg, pu = ps_g[gup % 2], ps_u[gup % 2]
                            for hc in range(HC):
                                te.matmul(
                                    pg[:, :], wg[slot][:, hc * P:(hc + 1) * P],
                                    lnx2T[:, hc * TB + th * 512:hc * TB + th * 512 + 512],
                                    start=(hc == 0), stop=(hc == HC - 1))
                            for hc in range(HC):
                                i = te.matmul(
                                    pu[:, :], wu[slot][:, hc * P:(hc + 1) * P],
                                    lnx2T[:, hc * TB + th * 512:hc * TB + th * 512 + 512],
                                    start=(hc == 0), stop=(hc == HC - 1))
                            pr.inc(i, "pe", 1, f"fguh_{gup}")
                            gup += 1
                        if ib == NIB - 1 and c == CPB - 1:
                            pr.mark_here("pe", f"fgu_last_{tb}")
                        ci += 1
                    pr.wait(te, "t", f"fdnw_{tb}_{ib}")
                    pr.wait(te, "t", ("ht", (tb * NIB + ib + 1) * 2 * CPB))
                    for j in range(TPB):
                        for half in range(2):
                            if dnp >= 2:
                                pr.wait(te, "t", f"fev_{dnp-2}")
                            pd = ps_d[dnp % 2]
                            for c in range(CPB):
                                for n in range(2):
                                    i = te.matmul(
                                        pd[:, n * 512:(n + 1) * 512],
                                        hT[:, c * TB + j * P:c * TB + (j + 1) * P],
                                        dn[:, c * H + half * 1024 + n * 512:
                                           c * H + half * 1024 + (n + 1) * 512],
                                        start=(c == 0), stop=(c == CPB - 1))
                            pr.inc(i, "pe", 1, f"fdn_{dnp}")
                            dnp += 1
                    pr.mark_here("pe", f"fdn_done_{tb}_{ib}")

        @block.scalar
        def _(a):
            pr.fence_wait(a, "a", "a3_1")
            gup = 0
            for tb in range(FFN_BLOCKS):
                for ib in range(NIB):
                    for c in range(CPB):
                        for th in range(2):
                            pr.wait(a, "a", f"fguh_{gup}")
                            if gup >= 2:
                                pr.wait(a, "a", f"fhev_{gup-2}")
                            i = a.activation(silu_g[:, th * 512:(th + 1) * 512],
                                             ps_g[gup % 2][:, :], AF.Silu)
                            pr.inc(i, "act", 1, f"fsilu_{gup}")
                            gup += 1
                            if (DEBUG2 and tb == 0 and ib == 0 and c == 1
                                    and th == 1):
                                pr.wait(a, "a", ("ht", 4))
                                pr.inc(a.dma_start(
                                    out=env["dbg"]["fht1"][:, :],
                                    in_=hT[:, TB:2 * TB]), "st", 16)
                                i = a.dma_start(out=env["dbg"]["fsilu"][:, :],
                                                in_=silu_g[:, :])
                                pr.inc(i, "st", 16, "dbg6_st")
                    if DEBUG2 and tb == 0 and ib == 0:
                        pr.wait(a, "a", "fw_1")
                        pr.inc(a.dma_start(out=env["dbg"]["fwg"][:, :],
                                           in_=wg[1][:, :]), "st", 16)
                        i = a.dma_start(out=env["dbg"]["fwu"][:, :],
                                        in_=wu[1][:, :])
                        pr.inc(i, "st", 16, "dbg5_st")
                        pr.wait(a, "a", ("ht", 2 * CPB))
                        pr.inc(a.dma_start(out=env["dbg"]["flnx"][:, :],
                                           in_=lnx2T[:, :]), "st", 16)
                        i = a.dma_start(out=env["dbg"]["fht"][:, :],
                                        in_=hT[:, :])
                        pr.inc(i, "st", 16, "dbg4_st")

        @block.vector
        def _(v):
            pr.fence_wait(v, "v", "a3_1")
            gup = 0
            dnp = 0
            for tb in range(FFN_BLOCKS):
                for ib in range(NIB):
                    for c in range(CPB):
                        for th in range(2):
                            pr.wait(v, "v", f"fsilu_{gup}")
                            if DEBUG2 and tb == 0 and ib == 0 and c == 2 and th == 0:
                                pr.wait(v, "v", "dbg6_st")
                            if DEBUG2 and tb == 0 and ib == 1 and c == 0 and th == 0:
                                pr.wait(v, "v", "dbg4_st")
                            if tb + ib > 0:
                                pr.wait(v, "v", f"fdn_done_{tb}_{ib-1}" if ib > 0
                                        else f"fdn_done_{tb-1}_{NIB-1}")
                            i = v.scalar_tensor_tensor(
                                hT[:, c * TB + th * 512:c * TB + (th + 1) * 512],
                                ps_u[gup % 2][:, :], 1.0,
                                silu_g[:, th * 512:(th + 1) * 512],
                                ALU.bypass, ALU.mult)
                            pr.inc(i, "ht", 1)
                            pr.mark_here("ht", f"fhev_{gup}")
                            gup += 1
                    for j in range(TPB):
                        gt = tb * TPB + j
                        for half in range(2):
                            pr.wait(v, "v", f"fdn_{dnp}")
                            if ib == 0 and half == 0:
                                pr.wait(v, "v", f"fx2_{gt}")
                                if tb >= 1:
                                    pr.wait(v, "v", f"fst_{tb-1}_{TPB-1}")
                            if ib < 4:
                                scal = sig_buf[:, gt:gt + 1]
                            else:
                                scal = comb_buf[:, gt * E + (ib - 4):
                                                gt * E + (ib - 4) + 1]
                            dst = outb[:, j * H + half * 1024:j * H + (half + 1) * 1024]
                            if ib == 0:
                                src1 = x2t[:, half * 1024:(half + 1) * 1024]
                            else:
                                src1 = dst
                            i = v.scalar_tensor_tensor(
                                dst, ps_d[dnp % 2][:, :], scal, src1,
                                ALU.mult, ALU.add)
                            pr.inc(i, "dve", 1, f"fev_{dnp}")
                            if ib == 0 and half == 1:
                                pr.mark_here("dve", f"fev0_{gt}")
                            if ib == NIB - 1 and half == 1:
                                pr.mark_here("dve", f"fevL_{tb}_{j}")
                            dnp += 1

        @block.sync
        def _(sp):
            tb = FFN_BLOCKS - 1
            for j in range(TPB):
                gt = tb * TPB + j
                pr.wait(sp, "sp", f"fevL_{tb}_{j}")
                i = sp.dma_start(out=out[gt * P:(gt + 1) * P, :],
                                 in_=outb[:, j * H:(j + 1) * H])
                pr.inc(i, "so", 16, f"fst_{tb}_{j}")

        pr.fence_record("ffn")


def make_consts():
    eyef = np.eye(P, dtype=np.float32)
    eyeb = np.eye(P).astype(ml_dtypes.bfloat16)
    zerob = np.zeros((P, P), dtype=ml_dtypes.bfloat16)
    mask = np.triu(np.full((P, P), MASK_NEG, dtype=np.float32), k=1)
    j = np.arange(0, DHEAD, 2, dtype=np.float32) / DHEAD
    freqs = (1.0 / 10000.0 ** j).astype(np.float32)
    t = np.arange(S, dtype=np.float32)
    ang = t[:, None] * freqs[None, :]
    cos = np.cos(ang).astype(np.float32)
    sin = np.sin(ang).astype(np.float32)
    ropec = np.concatenate(
        [cos * np.float32(SCALE), sin * np.float32(SCALE), cos, sin],
        axis=1).astype(ml_dtypes.bfloat16)
    return {
        "c_eyef": eyef, "c_eyeb": eyeb, "c_zerob": zerob, "c_mask": mask,
        "c_rope": np.ascontiguousarray(ropec),
    }


_CACHED = None


def get_program():
    global _CACHED
    if _CACHED is None:
        nc1, pr1 = build_program()
        nc2, pr2 = build_program(marks=pr1.marks)
        assert pr2.missing == 0
        _CACHED = nc2
    return _CACHED


def kernel(L0_attn_x, L0_attn_Wqkv, L0_attn_Wo, L0_ffn_W_router,
           L0_ffn_W_sgate, L0_ffn_Ws_gate_up, L0_ffn_Ws_down,
           L0_ffn_We_gate_up, L0_ffn_We_down):
    nc = get_program()
    consts = make_consts()
    x = np.ascontiguousarray(L0_attn_x, dtype=np.float32).reshape(16 * S, H)
    common = {
        "w_qkv": np.ascontiguousarray(L0_attn_Wqkv),
        "w_o": np.ascontiguousarray(L0_attn_Wo),
        "w_r": np.ascontiguousarray(L0_ffn_W_router),
        "w_sg": np.ascontiguousarray(L0_ffn_W_sgate),
        "w_sgu": np.ascontiguousarray(L0_ffn_Ws_gate_up),
        "w_sd": np.ascontiguousarray(L0_ffn_Ws_down),
        "w_egu": np.ascontiguousarray(L0_ffn_We_gate_up).reshape(E * H, 2 * I_MOE),
        "w_ed": np.ascontiguousarray(L0_ffn_We_down).reshape(E * I_MOE, H),
        **consts,
    }
    in_maps = []
    for c in range(NCORES):
        m = dict(common)
        m["x"] = np.ascontiguousarray(x[c * NB * S:(c + 1) * NB * S, :])
        in_maps.append(m)
    res = run_bass_kernel_spmd(nc, in_maps, core_ids=list(range(NCORES)))
    outs = [np.asarray(r["out"]) for r in res.results]
    return np.concatenate(outs, axis=0).reshape(16, S, H).astype(np.float32)


# version 10
# speedup vs baseline: 1.0809x; 1.0809x over previous
import numpy as np
import ml_dtypes
from contextlib import ExitStack

import concourse.bass as bass
import concourse.mybir as mybir
from concourse.bass_utils import run_bass_kernel_spmd

F32 = mybir.dt.float32
BF16 = mybir.dt.bfloat16
AF = mybir.ActivationFunctionType
ALU = mybir.AluOpType
AX = mybir.AxisListType

P = 128
H = 2048
S = 2048
NB = 2
NCORES = 8
NT = S // P
HC = H // P
NHEAD = 16
NKV = 4
DHEAD = 128
QKV = 3072
I_SH = 5632
I_MOE = 1408
E = 8
MASK_NEG = -2.3819763e38
EPS = 1e-5
SCALE = DHEAD ** -0.5

TB = 1024
NTB = (NB * S) // TB
TPB = TB // P
CPB = I_MOE // P
NIB = 4 + E

DEBUG = False
DEBUG2 = False
PHASES = "full"
FFN_BLOCKS = NTB
WSLOTS = 2

SEMS = ["ld", "lda", "ldb", "st", "so", "w", "w2", "wd", "pe", "dve", "act", "ht", "dn"]


class Prog:
    def __init__(self, nc, stack, marks=None):
        self.nc = nc
        self.final = marks is not None
        self.marks = dict(marks) if marks else {}
        self.sems = {}
        self.cnt = {}
        self.waited = {}
        self.missing = 0
        for s in SEMS:
            self.sems[s] = stack.enter_context(nc.semaphore(s))
            self.cnt[s] = 0

    def inc(self, instr, sem, k=1, label=None):
        instr.then_inc(self.sems[sem], k)
        self.cnt[sem] += k
        if label is not None:
            self._set(label, (sem, self.cnt[sem]))
        return (sem, self.cnt[sem])

    def _set(self, label, val):
        if not self.final:
            self.marks[label] = val
        else:
            assert self.marks.get(label) == val, (
                f"mark drift {label}: {self.marks.get(label)} vs {val}")

    def mark_here(self, sem, label):
        self._set(label, (sem, self.cnt[sem]))

    def wait(self, eng, engname, mark):
        if isinstance(mark, str):
            if mark not in self.marks:
                assert not self.final, f"unknown mark {mark}"
                self.missing += 1
                return
            mark = self.marks[mark]
        sem, v = mark
        if v <= 0:
            return
        key = (engname, sem)
        if self.waited.get(key, -1) >= v:
            return
        eng.wait_ge(self.sems[sem], v)
        self.waited[key] = v

    def fence_record(self, label):
        self._set("F_" + label, tuple((s, self.cnt[s]) for s in SEMS))

    def fence_wait(self, eng, engname, label):
        key = "F_" + label
        if key not in self.marks:
            assert not self.final
            self.missing += 1
            return
        for sem, v in self.marks[key]:
            self.wait(eng, engname, (sem, v))


def build_program(marks=None):
    nc = bass.Bass("TRN2", target_bir_lowering=False)

    x_in = nc.dram_tensor("x", [NB * S, H], F32, kind="ExternalInput").ap()
    w_qkv = nc.dram_tensor("w_qkv", [H, QKV], F32, kind="ExternalInput").ap()
    w_o = nc.dram_tensor("w_o", [H, H], F32, kind="ExternalInput").ap()
    w_r = nc.dram_tensor("w_r", [E, H], F32, kind="ExternalInput").ap()
    w_sg = nc.dram_tensor("w_sg", [1, H], F32, kind="ExternalInput").ap()
    w_sgu = nc.dram_tensor("w_sgu", [H, 2 * I_SH], F32, kind="ExternalInput").ap()
    w_sd = nc.dram_tensor("w_sd", [I_SH, H], F32, kind="ExternalInput").ap()
    w_egu = nc.dram_tensor("w_egu", [E * H, 2 * I_MOE], F32, kind="ExternalInput").ap()
    w_ed = nc.dram_tensor("w_ed", [E * I_MOE, H], F32, kind="ExternalInput").ap()
    c_eyef = nc.dram_tensor("c_eyef", [P, P], F32, kind="ExternalInput").ap()
    c_eyeb = nc.dram_tensor("c_eyeb", [P, P], BF16, kind="ExternalInput").ap()
    c_zerob = nc.dram_tensor("c_zerob", [P, P], BF16, kind="ExternalInput").ap()
    c_mask = nc.dram_tensor("c_mask", [P, P], F32, kind="ExternalInput").ap()
    c_rope = nc.dram_tensor("c_rope", [S, 256], BF16, kind="ExternalInput").ap()

    out = nc.dram_tensor("out", [NB * S, H], F32, kind="ExternalOutput").ap()

    qt_scr = nc.dram_tensor("qt_scr", [NB * NT * P, H], BF16, kind="Internal").ap()
    at_scr = nc.dram_tensor("at_scr", [NB * 4 * P, HC * 512], BF16, kind="Internal").ap()
    l2_scr = nc.dram_tensor("l2_scr", [NB * NT * P, H], BF16, kind="Internal").ap()

    dbg = {}
    if DEBUG:
        dbg["kT"] = nc.dram_tensor("dbg_kT", [P, NKV * S], BF16, kind="ExternalOutput").ap()
        dbg["v"] = nc.dram_tensor("dbg_v", [P, NT * 512], BF16, kind="ExternalOutput").ap()
        dbg["qt"] = nc.dram_tensor("dbg_qt", [NB * NT * P, H], BF16, kind="ExternalOutput").ap()
        dbg["at"] = nc.dram_tensor("dbg_at", [NB * 4 * P, HC * 512], BF16, kind="ExternalOutput").ap()
        dbg["l2"] = nc.dram_tensor("dbg_l2", [NB * NT * P, H], BF16, kind="ExternalOutput").ap()
        dbg["comb"] = nc.dram_tensor("dbg_comb", [P, NB * NT * E], F32, kind="ExternalOutput").ap()
        dbg["sg"] = nc.dram_tensor("dbg_sg", [P, NB * NT], F32, kind="ExternalOutput").ap()
    if DEBUG2:
        dbg["stat"] = nc.dram_tensor("dbg_stat", [P, 16], F32, kind="ExternalOutput").ap()
        dbg["lnx"] = nc.dram_tensor("dbg_lnx", [P, H], BF16, kind="ExternalOutput").ap()
        dbg["xt"] = nc.dram_tensor("dbg_xt", [P, H], F32, kind="ExternalOutput").ap()
        dbg["p8"] = nc.dram_tensor("dbg_p8", [P, 4 * E], F32, kind="ExternalOutput").ap()
        dbg["st2"] = nc.dram_tensor("dbg_st2", [P, 16], F32, kind="ExternalOutput").ap()
        dbg["flnx"] = nc.dram_tensor("dbg_flnx", [P, HC * TB], BF16, kind="ExternalOutput").ap()
        dbg["fht"] = nc.dram_tensor("dbg_fht", [P, CPB * TB], BF16, kind="ExternalOutput").ap()
        dbg["fwg"] = nc.dram_tensor("dbg_fwg", [P, HC * P], BF16, kind="ExternalOutput").ap()
        dbg["fht1"] = nc.dram_tensor("dbg_fht1", [P, TB], BF16, kind="ExternalOutput").ap()
        dbg["fsilu"] = nc.dram_tensor("dbg_fsilu", [P, 1024], F32, kind="ExternalOutput").ap()
        dbg["fwu"] = nc.dram_tensor("dbg_fwu", [P, HC * P], BF16, kind="ExternalOutput").ap()

    stack = ExitStack()
    with stack:
        pr = Prog(nc, stack, marks)

        def sb(st, name, shape, dt):
            return st.enter_context(nc.sbuf_tensor(name, shape, dt))

        def psum(st, name, shape, dt=F32):
            return st.enter_context(nc.psum_tensor(name, shape, dt))

        eyef = sb(stack, "eyef", [P, P], F32)
        eyeb = sb(stack, "eyeb", [P, P], BF16)
        zerob = sb(stack, "zerob", [P, P], BF16)
        maskd = sb(stack, "maskd", [P, P], F32)
        ropec = sb(stack, "ropec", [P, NT * 256], BF16)
        wrsT = sb(stack, "wrsT", [P, HC * 9], F32)
        sig_buf = sb(stack, "sig_buf", [P, NB * NT], F32)
        comb_buf = sb(stack, "comb_buf", [P, NB * NT * E], F32)
        stat = sb(stack, "stat", [P, 16], F32)
        p8 = sb(stack, "p8", [P, 4 * E], F32)

        def stc(i):
            return stat[:, i:i + 1]

        env = dict(x_in=x_in, w_qkv=w_qkv, w_o=w_o, w_sgu=w_sgu, w_sd=w_sd,
                   c_zerob_d=c_zerob, c_mask_d=c_mask,
                   w_egu=w_egu, w_ed=w_ed, out=out, qt_scr=qt_scr,
                   at_scr=at_scr, l2_scr=l2_scr, eyef=eyef, eyeb=eyeb,
                   zerob=zerob, maskd=maskd, ropec=ropec, wrsT=wrsT,
                   sig_buf=sig_buf, comb_buf=comb_buf, stat=stat, stc=stc,
                   p8=p8, dbg=dbg)

        block = stack.enter_context(nc.Block())

        @block.sync
        def _(sp):
            pr.inc(sp.dma_start(out=eyef[:, :], in_=c_eyef[:, :]), "ld", 16)
            pr.inc(sp.dma_start(out=eyeb[:, :], in_=c_eyeb[:, :]), "ld", 16)
            pr.inc(sp.dma_start(out=zerob[:, :], in_=c_zerob[:, :]), "ld", 16)
            pr.inc(sp.dma_start(out=maskd[:, :], in_=c_mask[:, :]), "ld", 16)
            i = sp.dma_start(out=ropec.rearrange("p (t f) -> p t f", f=256),
                             in_=c_rope.rearrange("(t p) f -> p t f", p=P))
            pr.inc(i, "ld", 16, "c_all")

        with ExitStack() as st0:
            wr_raw = sb(st0, "wr_raw", [P, H], F32)
            ps0 = psum(st0, "ps0", [P, HC * 9], F32)

            @block.sync
            def _(sp):
                pr.inc(sp.dma_start(out=wr_raw[0:E, :], in_=w_r[:, :]), "ld", 16)
                i = sp.dma_start(out=wr_raw[E:E + 1, :], in_=w_sg[:, :])
                pr.inc(i, "ld", 16, "wr_raw")

            @block.tensor
            def _(t):
                pr.wait(t, "t", "wr_raw")
                pr.wait(t, "t", "c_all")
                for c in range(HC):
                    i = t.transpose(ps0[:, c * 9:(c + 1) * 9],
                                    wr_raw[0:9, c * P:(c + 1) * P],
                                    eyef[0:9, 0:9])
                pr.inc(i, "pe", 1, "wrsT_ps")

            @block.vector
            def _(v):
                v.memset(stc(13), EPS)
                v.memset(stc(15), 1.0)
                v.memset(comb_buf[:, :], 0.0)
                v.memset(sig_buf[:, :], 0.0)
                pr.wait(v, "v", "wrsT_ps")
                i = v.tensor_copy(wrsT[:, :], ps0[:, :])
                pr.inc(i, "dve", 1, "wrsT_ok")

            pr.fence_record("p0")

        nseq = NB if PHASES == "full" else 1
        for s in range(nseq):
            with ExitStack() as stA:
                kT_sb = sb(stA, f"kT{s}", [P, NKV * S], BF16)
                v_sb = sb(stA, f"v{s}", [P, NT * 512], BF16)
                emit_A1(nc, pr, block, s, sb, psum, env, kT_sb, v_sb)
                if PHASES in ("a2", "a3", "full"):
                    emit_A2(nc, pr, block, s, sb, psum, env, kT_sb, v_sb)
            if PHASES in ("a3", "full"):
                emit_A3(nc, pr, block, s, sb, psum, env)

        if PHASES == "full":
            emit_FFN(nc, pr, block, sb, psum, env)

        if DEBUG:
            @block.sync
            def _(sp):
                if PHASES == "full":
                    pr.fence_wait(sp, "sp", "ffn")
                else:
                    pr.fence_wait(sp, "sp",
                                  {"a1": "a1_0", "a2": "a2_0", "a3": "a3_0"}[PHASES])
                if PHASES in ("a3", "full"):
                    pr.inc(sp.dma_start(out=dbg["comb"][:, :], in_=comb_buf[:, :]), "st", 16)
                    pr.inc(sp.dma_start(out=dbg["sg"][:, :], in_=sig_buf[:, :]), "st", 16)
                    pr.inc(sp.dma_start(out=dbg["l2"][:, :], in_=l2_scr[:, :]), "st", 16)
                if PHASES in ("a2", "a3", "full"):
                    pr.inc(sp.dma_start(out=dbg["at"][:, :], in_=at_scr[:, :]), "st", 16)
                i = sp.dma_start(out=dbg["qt"][:, :], in_=qt_scr[:, :])
                pr.inc(i, "st", 16, "dbg_done")

        @block.sync
        def _(sp):
            sp.wait_ge(pr.sems["st"], pr.cnt["st"])

    return nc, pr


def emit_A1(nc, pr, block, s, sb, psum, env, kT_sb, v_sb):
    x_in = env["x_in"]; w_qkv = env["w_qkv"]; qt_scr = env["qt_scr"]
    c_zerob_g = env["c_zerob_d"]
    eyeb = env["eyeb"]; ropec = env["ropec"]; stc = env["stc"]
    prev_fence = "p0" if s == 0 else f"a3_{s-1}"

    with ExitStack() as st1:
        wqkv_sb = sb(st1, f"wqkv{s}", [P, HC * QKV], BF16)
        x_tile = [sb(st1, f"x{s}_{i}", [P, H], F32) for i in range(2)]
        sq_scr = sb(st1, f"sq{s}", [P, H], F32)
        lnx_bf = sb(st1, f"lnx{s}", [P, H], BF16)
        lnxT_bf = sb(st1, f"lnxT{s}", [P, HC * P], BF16)
        q_tok = sb(st1, f"qtok{s}", [P, NHEAD * DHEAD], BF16)
        k_tok = sb(st1, f"ktok{s}", [P, NKV * DHEAD], BF16)
        ropet = sb(st1, f"ropet{s}", [P, 2048], F32)
        qT_tile = sb(st1, f"qTt{s}", [P, HC * P], BF16)
        ps_qkv = psum(st1, f"ps_qkv{s}", [P, QKV], F32)
        ps_tr = psum(st1, f"ps_tr{s}", [P, H], BF16)

        @block.gpsimd
        def _(g):
            pr.fence_wait(g, "g", prev_fence)
            i = g.dma_start(out=wqkv_sb.rearrange("p (c m) -> p c m", m=QKV),
                            in_=w_qkv.rearrange("(c p) m -> p c m", p=P))
            pr.inc(i, "wd", 16)
            i = g.dma_start(out=qT_tile[:, 0:P], in_=env["c_mask_d"][:, :])
            pr.inc(i, "wd", 16, f"a1w_{s}")

        @block.sync
        def _(sp):
            pr.fence_wait(sp, "sp", prev_fence)
            for t in range(NT):
                if t >= 2:
                    pr.wait(sp, "sp", f"a1ln_{s}_{t-2}")
                    pr.wait(sp, "sp", f"a1st_{s}_{t-2}")
                i = sp.dma_start(
                    out=x_tile[t % 2][:, :],
                    in_=x_in[(s * NT + t) * P:(s * NT + t + 1) * P, :])
                pr.inc(i, "lda" if t % 2 == 0 else "ldb", 16, f"a1x_{s}_{t}")

        @block.vector
        def _(v):
            pr.fence_wait(v, "v", prev_fence)
            for t in range(NT):
                pr.wait(v, "v", f"a1x_{s}_{t}")
                if DEBUG2 and s == 0 and t == 1:
                    pr.wait(v, "v", "dbg2_st")
                xt = x_tile[t % 2]
                v.reduce_sum(stc(0), xt[:, :], axis=AX.X)
                i = v.scalar_tensor_tensor(sq_scr[:, :], xt[:, :], 1.0,
                                           xt[:, :], ALU.bypass, ALU.mult,
                                           accum_out=stc(2))
                pr.inc(i, "dve", 1, f"a1st_{s}_{t}")
                pr.wait(v, "v", f"a1mn_{s}_{t}")
                i = v.scalar_tensor_tensor(stc(3), stc(2), 1.0 / H, stc(5),
                                           ALU.mult, ALU.add)
                pr.inc(i, "dve", 1, f"a1var_{s}_{t}")
                pr.wait(v, "v", f"a1rstd_{s}_{t}")
                i = v.tensor_scalar(stc(7), stc(1), stc(6), -1.0,
                                    ALU.mult, ALU.mult)
                pr.inc(i, "dve", 1, f"a1bias_{s}_{t}")

                pr.wait(v, "v", f"a1trl_{s}_{t}")
                i = v.tensor_copy(lnxT_bf[:, :], ps_tr[:, 0:HC * P])
                pr.inc(i, "dve", 1, f"a1lnxT_{s}_{t}")
                pr.wait(v, "v", f"a1qkv_{s}_{t}")
                if t >= 1:
                    pr.wait(v, "v", f"a1trq_{s}_{t-1}")
                    pr.wait(v, "v", f"a1trk_{s}_{t-1}")
                cq = ropec[:, t * 256 + 0:t * 256 + 64]
                sq = ropec[:, t * 256 + 64:t * 256 + 128]
                ck = ropec[:, t * 256 + 128:t * 256 + 192]
                sk = ropec[:, t * 256 + 192:t * 256 + 256]

                def rope_one(off, nh, cos, sin, dst):
                    xv = ps_qkv[:, off:off + nh * DHEAD].rearrange(
                        "p (h d) -> p h d", d=DHEAD)
                    x1 = xv[:, :, 0:64]
                    x2 = xv[:, :, 64:128]
                    cb = cos[:, None, :].broadcast_to([P, nh, 64])
                    sbb = sin[:, None, :].broadcast_to([P, nh, 64])
                    ta = ropet[:, 0:nh * 64].rearrange("p (h d) -> p h d", d=64)
                    tb = ropet[:, 1024:1024 + nh * 64].rearrange(
                        "p (h d) -> p h d", d=64)
                    dv = dst.rearrange("p (h d) -> p h d", d=DHEAD)
                    v.tensor_tensor(ta, x1, cb, ALU.mult)
                    v.tensor_tensor(tb, x2, sbb, ALU.mult)
                    v.tensor_sub(dv[:, :, 0:64], ta, tb)
                    v.tensor_tensor(ta, x1, sbb, ALU.mult)
                    v.tensor_tensor(tb, x2, cb, ALU.mult)
                    return v.tensor_add(dv[:, :, 64:128], ta, tb)

                i = rope_one(0, NHEAD, cq, sq, q_tok[:, :])
                pr.inc(i, "dve", 1, f"a1ropeq_{s}_{t}")
                i = rope_one(NHEAD * DHEAD, NKV, ck, sk, k_tok[:, :])
                pr.inc(i, "dve", 1, f"a1ropek_{s}_{t}")

                pr.wait(v, "v", f"a1trq_{s}_{t}")
                if t >= 1:
                    pr.wait(v, "v", f"a1qsp_{s}_{t-1}")
                i = v.tensor_copy(qT_tile[:, :], ps_tr[:, 0:HC * P])
                pr.inc(i, "dve", 1, f"a1qev_{s}_{t}")
                pr.wait(v, "v", f"a1trk_{s}_{t}")
                i = v.tensor_copy(
                    kT_sb.rearrange("p (c n) -> p c n", n=S)[:, :, t * P:(t + 1) * P],
                    ps_tr[:, 0:NKV * P].rearrange("p (c m) -> p c m", m=P))
                pr.inc(i, "dve", 1, f"a1kev_{s}_{t}")

        if DEBUG2 and s == 0:
            @block.gpsimd
            def _(gg):
                pr.wait(gg, "g", "a1rstd_0_0")
                pr.inc(gg.dma_start(out=env["dbg"]["stat"][:, :],
                                    in_=env["stat"][:, :]), "st", 16)
                pr.wait(gg, "g", "a1ln_0_0")
                pr.inc(gg.dma_start(out=env["dbg"]["lnx"][:, :],
                                    in_=lnx_bf[:, :]), "st", 16)
                i = gg.dma_start(out=env["dbg"]["xt"][:, :], in_=x_tile[0][:, :])
                pr.inc(i, "st", 16, "dbg2_st")

        @block.scalar
        def _(a):
            pr.fence_wait(a, "a", prev_fence)
            for t in range(NT):
                if DEBUG2 and s == 0 and t == 1:
                    pr.wait(a, "a", "dbg2_st")
                pr.wait(a, "a", f"a1st_{s}_{t}")
                a.activation(stc(1), stc(0), AF.Identity, scale=1.0 / H)
                a.activation(stc(4), stc(1), AF.Square)
                i = a.activation(stc(5), stc(4), AF.Identity, scale=-1.0)
                pr.inc(i, "act", 1, f"a1mn_{s}_{t}")
                pr.wait(a, "a", f"a1var_{s}_{t}")
                a.activation(stc(4), stc(3), AF.Ln, bias=stc(13))
                i = a.activation(stc(6), stc(4), AF.Exp, scale=-0.5)
                pr.inc(i, "act", 1, f"a1rstd_{s}_{t}")
                pr.wait(a, "a", f"a1bias_{s}_{t}")
                if t >= 1:
                    pr.wait(a, "a", f"a1trl_{s}_{t-1}")
                i = a.activation(lnx_bf[:, :], x_tile[t % 2][:, :], AF.Identity,
                                 bias=stc(7), scale=stc(6))
                pr.inc(i, "act", 1, f"a1ln_{s}_{t}")
                pr.wait(a, "a", f"a1qkv_{s}_{t}")
                i = a.copy(v_sb[:, t * 512:(t + 1) * 512],
                           ps_qkv[:, QKV - 512:QKV])
                pr.inc(i, "act", 1, f"a1v_{s}_{t}")
                pr.wait(a, "a", f"a1qev_{s}_{t}")
                i = a.dma_start(
                    out=qt_scr[(s * NT + t) * P:(s * NT + t + 1) * P, :],
                    in_=qT_tile[:, :])
                pr.inc(i, "st", 16, f"a1qsp_{s}_{t}")

        @block.tensor
        def _(te):
            pr.fence_wait(te, "t", prev_fence)
            pr.wait(te, "t", f"a1w_{s}")
            for t in range(NT):
                pr.wait(te, "t", f"a1ln_{s}_{t}")
                if t >= 1:
                    pr.wait(te, "t", f"a1kev_{s}_{t-1}")
                for c in range(HC):
                    i = te.transpose(ps_tr[:, c * P:(c + 1) * P],
                                     lnx_bf[:, c * P:(c + 1) * P], eyeb[:, :])
                pr.inc(i, "pe", 1, f"a1trl_{s}_{t}")
                pr.wait(te, "t", f"a1lnxT_{s}_{t}")
                if t >= 1:
                    pr.wait(te, "t", f"a1ropek_{s}_{t-1}")
                    pr.wait(te, "t", f"a1v_{s}_{t-1}")
                for n in range(QKV // 512):
                    for c in range(HC):
                        i = te.matmul(
                            ps_qkv[:, n * 512:(n + 1) * 512],
                            lnxT_bf[:, c * P:(c + 1) * P],
                            wqkv_sb[:, c * QKV + n * 512:c * QKV + (n + 1) * 512],
                            start=(c == 0), stop=(c == HC - 1))
                pr.inc(i, "pe", 1, f"a1qkv_{s}_{t}")
                pr.wait(te, "t", f"a1ropeq_{s}_{t}")
                for c in range(HC):
                    i = te.transpose(ps_tr[:, c * P:(c + 1) * P],
                                     q_tok[:, c * P:(c + 1) * P], eyeb[:, :])
                pr.inc(i, "pe", 1, f"a1trq_{s}_{t}")
                pr.wait(te, "t", f"a1ropek_{s}_{t}")
                pr.wait(te, "t", f"a1qev_{s}_{t}")
                for c in range(NKV):
                    i = te.transpose(ps_tr[:, c * P:(c + 1) * P],
                                     k_tok[:, c * P:(c + 1) * P], eyeb[:, :])
                pr.inc(i, "pe", 1, f"a1trk_{s}_{t}")

        if DEBUG and s == 0:
            @block.sync
            def _(sp):
                pr.wait(sp, "sp", f"a1kev_{s}_{NT-1}")
                pr.wait(sp, "sp", f"a1v_{s}_{NT-1}")
                pr.inc(sp.dma_start(out=env["dbg"]["kT"][:, :], in_=kT_sb[:, :]), "st", 16)
                i = sp.dma_start(out=env["dbg"]["v"][:, :], in_=v_sb[:, :])
                pr.inc(i, "st", 16, "dbg_a1")

        pr.fence_record(f"a1_{s}")


def emit_A2(nc, pr, block, s, sb, psum, env, kT_sb, v_sb):
    qt_scr = env["qt_scr"]; at_scr = env["at_scr"]
    eyef = env["eyef"]; zerob = env["zerob"]; maskd = env["maskd"]
    stc = env["stc"]
    NG = 4
    NSLOT = NG * NHEAD * 4

    def slots():
        g = 0
        for qg in range(NG):
            for h in range(NHEAD):
                for j in range(4):
                    yield qg, h, j, g
                    g += 1

    def hslot(qg, h):
        return qg * NHEAD + h

    def slot_of(g):
        qg, r = divmod(g, NHEAD * 4)
        h, j = divmod(r, 4)
        return qg, h, j

    def cmx(g):
        return stc(8) if g % 2 == 0 else stc(12)

    def crc(g):
        return stc(10) if g % 2 == 0 else stc(14)

    with ExitStack() as st2:
        qTg = sb(st2, f"qTg{s}", [P, HC * 512], BF16)
        scs = [sb(st2, f"scs{s}_{i}", [P, S], BF16) for i in range(2)]
        probs_sb = [sb(st2, f"probs{s}_{i}", [P, S], BF16) for i in range(2)]
        probsT = [sb(st2, f"pT{s}_{i}", [P, NT * 512], BF16) for i in range(2)]
        attnTg = sb(st2, f"aTg{s}", [P, HC * 512], BF16)
        dmat = [sb(st2, f"dm{s}_{i}", [P, P], BF16) for i in range(2)]
        ps_sc = psum(st2, f"ps_sc{s}", [P, S], F32)
        ps_pt = [psum(st2, f"ps_pt{s}_{i}", [P, 512], F32) for i in range(2)]
        ps_pv = [psum(st2, f"ps_pv{s}_{i}", [P, 512], F32) for i in range(2)]

        @block.sync
        def _(sp):
            pr.fence_wait(sp, "sp", f"a1_{s}")
            for qg in range(NG):
                if qg >= 1:
                    pr.wait(sp, "sp", f"a2qk_{s}_{qg * NHEAD * 4 - 1}")
                for jj in range(4):
                    gq = (s * NT + qg * 4 + jj) * P
                    i = sp.dma_start(
                        out=qTg[:, jj * H:(jj + 1) * H],
                        in_=qt_scr[gq:gq + P, :])
                    pr.inc(i, "ld", 16, f"a2q_{s}_{qg}" if jj == 3 else None)

        @block.tensor
        def _(t):
            pr.fence_wait(t, "t", f"a1_{s}")
            ptb = 0

            def emit_dmm(g):
                nonlocal ptb
                qg, h, j = slot_of(g)
                qi = qg * 4 + j
                kmax = (qi + 1) * P
                nkc = 4 * (qg + 1)
                pr.wait(t, "t", f"a2dm_{s}_{g}")
                pr.wait(t, "t", f"a2exp_{s}_{g}")
                nb4 = (nkc + 3) // 4
                for b4 in range(nb4):
                    if ptb >= 2:
                        pr.wait(t, "t", f"a2ptev_{s}_{ptb-2}")
                        pass
                    pb = ps_pt[ptb % 2]
                    n4 = min(4, nkc - b4 * 4)
                    for k4 in range(n4):
                        kc = b4 * 4 + k4
                        live = kc * P < kmax
                        i = t.matmul(
                            pb[:, k4 * P:(k4 + 1) * P],
                            probs_sb[g % 2][:, kc * P:(kc + 1) * P] if live
                            else zerob[:, :],
                            dmat[j % 2][:, :] if live else zerob[:, :],
                            start=True, stop=True)
                    pr.inc(i, "pe", 1, f"a2pt_{s}_{ptb}")
                    ptb += 1
                pr.mark_here("pe", f"a2ptmm_{s}_{g}")
                if j == 3:
                    hs = hslot(qg, h)
                    kv = h // 4
                    pr.wait(t, "t", f"a2ptall_v_{s}_{hs}")
                    pr.wait(t, "t", f"a2ptall_a_{s}_{hs}")
                    if hs >= 2:
                        pr.wait(t, "t", f"a2pvev_{s}_{hs-2}")
                    for kc in range(nkc):
                        i = t.matmul(
                            ps_pv[h % 2][:, :],
                            v_sb[:, kc * 512 + kv * P:kc * 512 + (kv + 1) * P],
                            probsT[h % 2][:, kc * 512:(kc + 1) * 512],
                            start=(kc == 0), stop=(kc == nkc - 1))
                    pr.inc(i, "pe", 1, f"a2pv_{s}_{hs}")

            for qg, h, j, g in slots():
                kv = h // 4
                qi = qg * 4 + j
                kmax = (qi + 1) * P
                if j == 0 and h == 0:
                    pr.wait(t, "t", f"a2q_{s}_{qg}")
                if g >= 1:
                    pr.wait(t, "t", f"a2sev_{s}_{g-1}")
                lhs = qTg[:, j * H + h * P:j * H + (h + 1) * P]
                nfull, rem = kmax // 512, kmax % 512
                for c in range(nfull):
                    i = t.matmul(ps_sc[:, c * 512:(c + 1) * 512], lhs,
                                 kT_sb[:, kv * S + c * 512:kv * S + (c + 1) * 512],
                                 start=True, stop=True)
                if rem:
                    i = t.matmul(ps_sc[:, nfull * 512:nfull * 512 + rem], lhs,
                                 kT_sb[:, kv * S + nfull * 512:kv * S + nfull * 512 + rem],
                                 start=True, stop=True)
                pr.inc(i, "pe", 1, f"a2qk_{s}_{g}")
                if g >= 2:
                    emit_dmm(g - 2)
            emit_dmm(NSLOT - 2)
            emit_dmm(NSLOT - 1)

        def evac_batches(g, engine, engname, want_parity, sem):
            qg2, h2, j2 = slot_of(g)
            nkc2 = 4 * (qg2 + 1)
            nb42 = (nkc2 + 3) // 4
            base_ptb = sum((4 * (slot_of(x)[0] + 1) + 3) // 4 for x in range(g))
            for b4 in range(nb42):
                ptb = base_ptb + b4
                if ptb % 2 == want_parity:
                    pr.wait(engine, engname, f"a2pt_{s}_{ptb}")
                    n4 = min(4, nkc2 - b4 * 4)
                    if engname == "a":
                        i = engine.copy(
                            probsT[h2 % 2].rearrange("p (c n) -> p c n", n=512)[
                                :, b4 * 4:b4 * 4 + n4, j2 * P:(j2 + 1) * P],
                            ps_pt[ptb % 2].rearrange("p (c n) -> p c n", n=P)[
                                :, 0:n4, :])
                        pr.inc(i, sem, 1, f"a2ptev_{s}_{ptb}")
                    else:
                        i = engine.tensor_copy(
                            probsT[h2 % 2].rearrange("p (c n) -> p c n", n=512)[
                                :, b4 * 4:b4 * 4 + n4, j2 * P:(j2 + 1) * P],
                            ps_pt[ptb % 2].rearrange("p (c n) -> p c n", n=P)[
                                :, 0:n4, :])
                        pr.inc(i, sem, 1, f"a2ptev_{s}_{ptb}")
            if j2 == 3:
                pr.mark_here(sem, f"a2ptall_{engname}_{s}_{hslot(qg2, h2)}")

        @block.scalar
        def _(a):
            pr.fence_wait(a, "a", f"a1_{s}")

            def emit_sev(g):
                qg2, h2, j2 = slot_of(g)
                km = (qg2 * 4 + j2 + 1) * P
                pr.wait(a, "a", f"a2qk_{s}_{g}")
                i = a.activation(scs[g % 2][:, 0:km], ps_sc[:, 0:km],
                                 AF.Identity)
                pr.inc(i, "act", 1, f"a2sev_{s}_{g}")

            def act_evac(gx):
                evac_batches(gx, a, "a", 1, "act")

            def act_pvev(gx):
                qg2, h2, j2 = slot_of(gx)
                if j2 != 3:
                    return
                hs2 = hslot(qg2, h2)
                pr.wait(a, "a", f"a2pv_{s}_{hs2}")
                if qg2 >= 1:
                    pr.wait(a, "a", f"a2asp_{s}_{qg2-1}")
                i2 = a.copy(attnTg[:, h2 * 512:(h2 + 1) * 512],
                            ps_pv[h2 % 2][:, :])
                pr.inc(i2, "act", 1, f"a2pvev_{s}_{hs2}")
                if h2 == NHEAD - 1:
                    i2 = a.dma_start(
                        out=at_scr[(s * 4 + qg2) * P:(s * 4 + qg2 + 1) * P, :],
                        in_=attnTg[:, :])
                    pr.inc(i2, "st", 16, f"a2asp_{s}_{qg2}")

            emit_sev(0)
            for qg, h, j, g in slots():
                qi = qg * 4 + j
                kmax = (qi + 1) * P
                if g >= 2:
                    act_evac(g - 2)
                if g >= 3:
                    act_pvev(g - 3)
                if g + 1 < NSLOT:
                    emit_sev(g + 1)
                pr.wait(a, "a", f"a2mx_{s}_{g}")
                if g >= 2:
                    pr.wait(a, "a", f"a2ptmm_{s}_{g-2}")
                i = a.activation(probs_sb[g % 2][:, 0:kmax],
                                 scs[g % 2][:, 0:kmax],
                                 AF.Exp, bias=cmx(g), accum_out=stc(9))
                pr.inc(i, "act", 1, f"a2exp_{s}_{g}")
                a.activation(stc(11), stc(9), AF.Ln)
                if g >= 2:
                    pr.wait(a, "a", f"a2dm_{s}_{g-2}")
                i = a.activation(crc(g), stc(11), AF.Exp, scale=-1.0)
                pr.inc(i, "act", 1, f"a2rcp_{s}_{g}")
                if g == NSLOT - 1:
                    act_evac(NSLOT - 2)
                    act_evac(NSLOT - 1)
                    act_pvev(NSLOT - 3)
                    act_pvev(NSLOT - 2)
                    act_pvev(NSLOT - 1)

        @block.vector
        def _(v):
            pr.fence_wait(v, "v", f"a1_{s}")

            def emit_dm(g):
                qg2, h2, j2 = slot_of(g)
                pr.wait(v, "v", f"a2rcp_{s}_{g}")
                if g >= 2:
                    pr.wait(v, "v", f"a2ptmm_{s}_{g-2}")
                i = v.tensor_scalar(dmat[j2 % 2][:, :], eyef[:, :], crc(g),
                                    None, ALU.mult)
                pr.inc(i, "dve", 1, f"a2dm_{s}_{g}")
                hs = hslot(qg2, h2)
                if j2 == 0 and hs >= 2:
                    pr.wait(v, "v", f"a2pv_{s}_{hs-2}")

            for qg, h, j, g in slots():
                qi = qg * 4 + j
                kmax = (qi + 1) * P
                pr.wait(v, "v", f"a2sev_{s}_{g}")
                sc = scs[g % 2]
                v.tensor_add(sc[:, kmax - P:kmax], sc[:, kmax - P:kmax],
                             maskd[:, :])
                if g >= 2:
                    pr.wait(v, "v", f"a2exp_{s}_{g-2}")
                i = v.reduce_max(cmx(g), sc[:, 0:kmax], axis=AX.X, negate=True)
                pr.inc(i, "dve", 1, f"a2mx_{s}_{g}")
                if g >= 1:
                    emit_dm(g - 1)
                if g >= 2:
                    evac_batches(g - 2, v, "v", 0, "dve")
            emit_dm(NSLOT - 1)
            evac_batches(NSLOT - 2, v, "v", 0, "dve")
            evac_batches(NSLOT - 1, v, "v", 0, "dve")

        pr.fence_record(f"a2_{s}")


def g3(s, t):
    return s * NT + t


def emit_A3(nc, pr, block, s, sb, psum, env):
    x_in = env["x_in"]; w_o = env["w_o"]; out = env["out"]
    at_scr = env["at_scr"]; l2_scr = env["l2_scr"]
    wrsT = env["wrsT"]; eyef = env["eyef"]
    sig_buf = env["sig_buf"]; comb_buf = env["comb_buf"]
    stc = env["stc"]; p8 = env["p8"]

    with ExitStack() as st3:
        wo_sb = sb(st3, f"wo{s}", [P, HC * H], BF16)
        atTg = sb(st3, f"atTg{s}", [P, HC * 512], BF16)
        x_tile = [sb(st3, f"x3_{s}_{i}", [P, H], F32) for i in range(2)]
        x2_sb = sb(st3, f"x2_{s}", [P, H], F32)
        sq3 = sb(st3, f"sq3_{s}", [P, H], F32)
        lnx2_f = sb(st3, f"l2f_{s}", [P, H], F32)
        lnx2T_f = sb(st3, f"l2Tf_{s}", [P, HC * P], F32)
        lnx2T_b = sb(st3, f"l2Tb_{s}", [P, HC * P], BF16)
        ps_o = psum(st3, f"ps_o{s}", [P, H], F32)
        ps_t2 = psum(st3, f"ps_t2{s}", [P, 1024], F32)
        ps_r = psum(st3, f"ps_r{s}", [P, 9], F32)

        @block.gpsimd
        def _(g):
            pr.fence_wait(g, "g", f"a2_{s}")
            pr.inc(g.dma_start(out=wo_sb.rearrange("p (c m) -> p c m", m=H),
                               in_=w_o.rearrange("(c p) m -> p c m", p=P)),
                   "wd", 16)
            i = g.dma_start(out=lnx2T_b[:, 0:P], in_=env["c_mask_d"][:, :])
            pr.inc(i, "wd", 16, f"a3w_{s}")
            if DEBUG2 and s == 0:
                pr.wait(g, "g", "a3cmb_0_0")
                pr.wait(g, "g", "a3sg_0_0")
                pr.inc(g.dma_start(out=env["dbg"]["p8"][:, :],
                                   in_=p8[:, :]), "st", 16)
                i = g.dma_start(out=env["dbg"]["st2"][:, :],
                                in_=env["stat"][:, :])
                pr.inc(i, "st", 16, "dbg3_st")

        @block.sync
        def _(sp):
            pr.fence_wait(sp, "sp", f"a2_{s}")
            for qg in range(4):
                if qg >= 1:
                    pr.wait(sp, "sp", f"a3wo_{s}_{qg-1}_3")
                i = sp.dma_start(
                    out=atTg[:, :],
                    in_=at_scr[(s * 4 + qg) * P:(s * 4 + qg + 1) * P, :])
                pr.inc(i, "ld", 16, f"a3at_{s}_{qg}")
                for j in range(4):
                    t = qg * 4 + j
                    if t >= 2:
                        pr.wait(sp, "sp", f"a3x2_{s}_{t-2}")
                    i = sp.dma_start(
                        out=x_tile[t % 2][:, :],
                        in_=x_in[(s * NT + t) * P:(s * NT + t + 1) * P, :])
                    pr.inc(i, "lda" if t % 2 == 0 else "ldb", 16, f"a3x_{s}_{t}")

        @block.tensor
        def _(te):
            pr.fence_wait(te, "t", f"a2_{s}")
            pr.wait(te, "t", f"a3w_{s}")

            def emit_tr_rt(t):
                for half in range(2):
                    pr.wait(te, "t", f"a3l2_{s}_{t}")
                    if t >= 1 or half >= 1:
                        pr.wait(te, "t", f"a3tev_{s}_{t}_{half-1}"
                                if half >= 1 else f"a3tev_{s}_{t-1}_1")
                    for c in range(8):
                        i = te.transpose(
                            ps_t2[:, c * P:(c + 1) * P],
                            lnx2_f[:, (half * 8 + c) * P:(half * 8 + c + 1) * P],
                            eyef[:, :])
                    pr.inc(i, "pe", 1, f"a3tr_{s}_{t}_{half}")
                pr.wait(te, "t", f"a3tev_{s}_{t}_1")
                if t >= 1:
                    pr.wait(te, "t", f"a3r8_{s}_{t-1}")
                for c in range(HC):
                    i = te.matmul(ps_r[:, :],
                                  lnx2T_f[:, c * P:(c + 1) * P],
                                  wrsT[:, c * 9:(c + 1) * 9],
                                  start=(c == 0), stop=(c == HC - 1))
                pr.inc(i, "pe", 1, f"a3rt_{s}_{t}")

            for qg in range(4):
                pr.wait(te, "t", f"a3at_{s}_{qg}")
                for j in range(4):
                    t = qg * 4 + j
                    if t >= 1:
                        pr.wait(te, "t", f"a3x2_{s}_{t-1}")
                    for c in range(HC):
                        for n in range(4):
                            i = te.matmul(
                                ps_o[:, n * 512:(n + 1) * 512],
                                atTg[:, c * 512 + j * P:c * 512 + (j + 1) * P],
                                wo_sb[:, c * H + n * 512:c * H + (n + 1) * 512],
                                start=(c == 0), stop=(c == HC - 1))
                    pr.inc(i, "pe", 1, f"a3wo_{s}_{qg}_{j}")
                    if t >= 1:
                        emit_tr_rt(t - 1)
            emit_tr_rt(NT - 1)

        @block.vector
        def _(v):
            pr.fence_wait(v, "v", f"a2_{s}")
            for t in range(NT):
                pr.wait(v, "v", f"a3wo_{s}_{t//4}_{t%4}")
                pr.wait(v, "v", f"a3x_{s}_{t}")
                if t >= 1:
                    pr.wait(v, "v", f"a3l2_{s}_{t-1}")
                    pr.wait(v, "v", f"a3xsp_{s}_{t-1}")
                i = v.tensor_add(x2_sb[:, :], ps_o[:, :], x_tile[t % 2][:, :])
                pr.inc(i, "dve", 1, f"a3x2_{s}_{t}")
                v.reduce_sum(stc(0), x2_sb[:, :], axis=AX.X)
                i = v.scalar_tensor_tensor(sq3[:, :], x2_sb[:, :], 1.0,
                                           x2_sb[:, :], ALU.bypass, ALU.mult,
                                           accum_out=stc(2))
                pr.inc(i, "dve", 1, f"a3st_{s}_{t}")
                pr.wait(v, "v", f"a3mn_{s}_{t}")
                i = v.scalar_tensor_tensor(stc(3), stc(2), 1.0 / H, stc(5),
                                           ALU.mult, ALU.add)
                pr.inc(i, "dve", 1, f"a3var_{s}_{t}")
                pr.wait(v, "v", f"a3rstd_{s}_{t}")
                i = v.tensor_scalar(stc(7), stc(1), stc(6), -1.0,
                                    ALU.mult, ALU.mult)
                pr.inc(i, "dve", 1, f"a3bias_{s}_{t}")
                for half in range(2):
                    pr.wait(v, "v", f"a3tr_{s}_{t}_{half}")
                    v.tensor_copy(lnx2T_f[:, half * 1024:(half + 1) * 1024],
                                  ps_t2[:, :])
                    i = v.tensor_copy(lnx2T_b[:, half * 1024:(half + 1) * 1024],
                                      ps_t2[:, :])
                    pr.inc(i, "dve", 1, f"a3tev_{s}_{t}_{half}")
                g = s * NT + t
                pr.wait(v, "v", f"a3rt_{s}_{t}")
                i = v.reduce_max(stc(8), ps_r[:, 0:E], axis=AX.X, negate=True)
                pr.inc(i, "dve", 1, f"a3rmx_{s}_{t}")
                pr.wait(v, "v", f"a3rcp_{s}_{t}")
                i = v.tensor_scalar(p8[:, 0:E], p8[:, 2 * E:3 * E], stc(10),
                                    None, ALU.mult)
                pr.inc(i, "dve", 1, f"a3rnm_{s}_{t}")
                i = v.max(out=p8[:, E:2 * E], in_=p8[:, 2 * E:3 * E])
                pr.inc(i, "dve", 1, f"a3srt_{s}_{t}")
                pr.wait(v, "v", f"a3pb_{s}_{t}")
                i = v.scalar_tensor_tensor(
                    comb_buf[:, g * E:(g + 1) * E], p8[:, 2 * E:3 * E],
                    stc(14), p8[:, 3 * E:4 * E], ALU.is_ge, ALU.mult)
                pr.inc(i, "dve", 1, f"a3cmb_{s}_{t}")

        @block.scalar
        def _(a):
            pr.fence_wait(a, "a", f"a2_{s}")
            for t in range(NT):
                pr.wait(a, "a", f"a3st_{s}_{t}")
                a.activation(stc(1), stc(0), AF.Identity, scale=1.0 / H)
                a.activation(stc(4), stc(1), AF.Square)
                i = a.activation(stc(5), stc(4), AF.Identity, scale=-1.0)
                pr.inc(i, "act", 1, f"a3mn_{s}_{t}")
                pr.wait(a, "a", f"a3var_{s}_{t}")
                a.activation(stc(4), stc(3), AF.Ln, bias=stc(13))
                i = a.activation(stc(6), stc(4), AF.Exp, scale=-0.5)
                pr.inc(i, "act", 1, f"a3rstd_{s}_{t}")
                pr.wait(a, "a", f"a3bias_{s}_{t}")
                if t >= 1:
                    pr.wait(a, "a", f"a3tr_{s}_{t-1}_1")
                i = a.activation(lnx2_f[:, :], x2_sb[:, :], AF.Identity,
                                 bias=stc(7), scale=stc(6))
                pr.inc(i, "act", 1, f"a3l2_{s}_{t}")
                pr.wait(a, "a", f"a3rmx_{s}_{t}")
                if DEBUG2 and s == 0 and t == 1:
                    pr.wait(a, "a", "dbg3_st")
                if t >= 1:
                    pr.wait(a, "a", f"a3cmb_{s}_{t-1}")
                i = a.activation(p8[:, 2 * E:3 * E], ps_r[:, 0:E], AF.Exp,
                                 bias=stc(8), accum_out=stc(9))
                pr.inc(i, "act", 1, f"a3rex_{s}_{t}")
                a.activation(stc(11), stc(9), AF.Ln)
                i = a.activation(stc(10), stc(11), AF.Exp, scale=-1.0)
                pr.inc(i, "act", 1, f"a3rcp_{s}_{t}")
                a.activation(stc(12), ps_r[:, E:E + 1], AF.Exp, scale=-1.0)
                a.activation(stc(11), stc(12), AF.Ln, bias=stc(15))
                i = a.activation(sig_buf[:, g3(s, t):g3(s, t) + 1], stc(11),
                                 AF.Exp, scale=-1.0)
                pr.inc(i, "act", 1, f"a3sg_{s}_{t}")
                pr.mark_here("act", f"a3r8_{s}_{t}")
                pr.wait(a, "a", f"a3srt_{s}_{t}")
                pr.wait(a, "a", f"a3rnm_{s}_{t}")
                a.activation(stc(14), p8[:, E + 1:E + 2], AF.Identity)
                i = a.activation(p8[:, 3 * E:4 * E], p8[:, 0:E], AF.Identity)
                pr.inc(i, "act", 1, f"a3pb_{s}_{t}")
                pr.wait(a, "a", f"a3x2_{s}_{t}")
                i = a.dma_start(out=out[(s * NT + t) * P:(s * NT + t + 1) * P, :],
                                in_=x2_sb[:, :])
                pr.inc(i, "so", 16, f"a3xsp_{s}_{t}")
                pr.wait(a, "a", f"a3tev_{s}_{t}_1")
                i = a.dma_start(
                    out=l2_scr[(s * NT + t) * P:(s * NT + t + 1) * P, :],
                    in_=lnx2T_b[:, :])
                pr.inc(i, "st", 16, f"a3lsp_{s}_{t}")

        pr.fence_record(f"a3_{s}")


def emit_FFN(nc, pr, block, sb, psum, env):
    c_zerob_g = env["c_zerob_d"]
    c_mask_g = env["c_mask_d"]
    out = env["out"]; l2_scr = env["l2_scr"]
    sig_buf = env["sig_buf"]; comb_buf = env["comb_buf"]
    w_sgu = env["w_sgu"]; w_sd = env["w_sd"]
    w_egu = env["w_egu"]; w_ed = env["w_ed"]

    with ExitStack() as st4:
        lnx2T = sb(st4, "f_lnx2T", [P, HC * TB], BF16)
        hT = sb(st4, "f_hT", [P, CPB * TB], BF16)
        wg = [sb(st4, f"f_wg{i}", [P, HC * P], BF16) for i in range(2)]
        wu = [sb(st4, f"f_wu{i}", [P, HC * P], BF16) for i in range(2)]
        dn = sb(st4, "f_dn", [P, CPB * H], BF16)
        outb = sb(st4, "f_out", [P, TPB * H], F32)
        x2t = sb(st4, "f_x2", [P, H], F32)
        silu_g = sb(st4, "f_silu", [P, 1024], F32)
        wfence = sb(st4, "f_wfence", [P, P], BF16)
        ps_g = [psum(st4, f"f_psg{i}", [P, 512], F32) for i in range(2)]
        ps_u = [psum(st4, f"f_psu{i}", [P, 512], F32) for i in range(2)]
        ps_d = [psum(st4, f"f_psd{i}", [P, 1024], F32) for i in range(2)]

        def gu_slices(ib, c):
            if ib < 4:
                i0 = ib * I_MOE + c * P
                return (w_sgu[:, i0:i0 + P], w_sgu[:, I_SH + i0:I_SH + i0 + P])
            e = ib - 4
            i0 = c * P
            return (w_egu[e * H:(e + 1) * H, i0:i0 + P],
                    w_egu[e * H:(e + 1) * H, I_MOE + i0:I_MOE + i0 + P])

        def dn_slice(ib, c):
            if ib < 4:
                r0 = ib * I_MOE + c * P
                return w_sd[r0:r0 + P, :]
            e = ib - 4
            r0 = e * I_MOE + c * P
            return w_ed[r0:r0 + P, :]

        @block.sync
        def _(sp):
            pr.fence_wait(sp, "sp", "a3_1")
            for tb in range(FFN_BLOCKS):
                if tb >= 1:
                    for j in range(TPB):
                        gtp = (tb - 1) * TPB + j
                        pr.wait(sp, "sp", f"fevL_{tb-1}_{j}")
                        i = sp.dma_start(out=out[gtp * P:(gtp + 1) * P, :],
                                         in_=outb[:, j * H:(j + 1) * H])
                        pr.inc(i, "so", 16, f"fst_{tb-1}_{j}")
                    pr.wait(sp, "sp", f"fgu_last_{tb-1}")
                for jj in range(TPB):
                    gl = (tb * TPB + jj) * P
                    i = sp.dma_start(
                        out=lnx2T[:, jj * H:(jj + 1) * H],
                        in_=l2_scr[gl:gl + P, :])
                    pr.inc(i, "ld", 16, f"fln_{tb}" if jj == TPB - 1 else None)
                for j in range(TPB):
                    gt = tb * TPB + j
                    if gt >= 1:
                        pr.wait(sp, "sp", f"fev0_{gt-1}")
                    i = sp.dma_start(out=x2t[:, :],
                                     in_=out[gt * P:(gt + 1) * P, :])
                    pr.inc(i, "lda", 16, f"fx2_{gt}")

        @block.gpsimd
        def _(g):
            pr.fence_wait(g, "g", "a3_1")
            ci = 0
            for tb in range(FFN_BLOCKS):
                for ib in range(NIB):
                    for c in range(CPB):
                        slot = ci % WSLOTS
                        if ci >= WSLOTS:
                            pr.wait(g, "g", f"fguh_{2*(ci-WSLOTS)+1}")
                        gate, up = gu_slices(ib, c)
                        wsem = "w" if slot == 0 else "w2"
                        if DEBUG2 and ci == 3:
                            pr.wait(g, "g", "dbg5_st")
                        pr.inc(g.dma_start(
                            out=wg[slot].rearrange("p (c m) -> p c m", m=P),
                            in_=gate.rearrange("(c p) m -> p c m", p=P)), wsem, 16)
                        pr.inc(g.dma_start(
                            out=wu[slot].rearrange("p (c m) -> p c m", m=P),
                            in_=up.rearrange("(c p) m -> p c m", p=P)), wsem, 16)
                        i = g.dma_start(out=wfence[:, :], in_=c_mask_g[:, :])
                        pr.inc(i, wsem, 16, f"fw_{ci}")
                        ci += 1
                    if tb + ib > 0:
                        pr.wait(g, "g", f"fdn_done_{tb}_{ib-1}" if ib > 0
                                else f"fdn_done_{tb-1}_{NIB-1}")
                    for c in range(CPB):
                        pr.inc(g.dma_start(out=dn[:, c * H:(c + 1) * H],
                                           in_=dn_slice(ib, c)), "wd", 16)
                    i = g.dma_start(out=wfence[:, :], in_=c_mask_g[:, :])
                    pr.inc(i, "wd", 16, f"fdnw_{tb}_{ib}")

        @block.tensor
        def _(te):
            pr.fence_wait(te, "t", "a3_1")
            ci = 0
            gup = 0
            dnp = 0
            for tb in range(FFN_BLOCKS):
                pr.wait(te, "t", f"fln_{tb}")
                for ib in range(NIB):
                    for c in range(CPB):
                        pr.wait(te, "t", f"fw_{ci}")
                        slot = ci % WSLOTS
                        if tb + ib > 0:
                            pr.wait(te, "t", f"fdn_done_{tb}_{ib-1}" if ib > 0
                                    else f"fdn_done_{tb-1}_{NIB-1}")
                        for th in range(2):
                            if gup >= 2:
                                pr.wait(te, "t", f"fhev_{gup-2}")
                            pg, pu = ps_g[gup % 2], ps_u[gup % 2]
                            lv = lnx2T.rearrange("p (j c m) -> p j c m",
                                                 j=TPB, m=P)
                            for hc in range(HC):
                                te.matmul(
                                    pg[:, :], wg[slot][:, hc * P:(hc + 1) * P],
                                    lv[:, 4 * th:4 * th + 4, hc, :],
                                    start=(hc == 0), stop=(hc == HC - 1))
                            for hc in range(HC):
                                i = te.matmul(
                                    pu[:, :], wu[slot][:, hc * P:(hc + 1) * P],
                                    lv[:, 4 * th:4 * th + 4, hc, :],
                                    start=(hc == 0), stop=(hc == HC - 1))
                            pr.inc(i, "pe", 1, f"fguh_{gup}")
                            gup += 1
                        if ib == NIB - 1 and c == CPB - 1:
                            pr.mark_here("pe", f"fgu_last_{tb}")
                        ci += 1
                    pr.wait(te, "t", f"fdnw_{tb}_{ib}")
                    pr.wait(te, "t", ("ht", (tb * NIB + ib + 1) * 2 * CPB))
                    for j in range(TPB):
                        for half in range(2):
                            if dnp >= 2:
                                pr.wait(te, "t", f"fev_{dnp-2}")
                            pd = ps_d[dnp % 2]
                            for c in range(CPB):
                                for n in range(2):
                                    i = te.matmul(
                                        pd[:, n * 512:(n + 1) * 512],
                                        hT[:, c * TB + j * P:c * TB + (j + 1) * P],
                                        dn[:, c * H + half * 1024 + n * 512:
                                           c * H + half * 1024 + (n + 1) * 512],
                                        start=(c == 0), stop=(c == CPB - 1))
                            pr.inc(i, "pe", 1, f"fdn_{dnp}")
                            dnp += 1
                    pr.mark_here("pe", f"fdn_done_{tb}_{ib}")

        @block.scalar
        def _(a):
            pr.fence_wait(a, "a", "a3_1")
            gup = 0
            for tb in range(FFN_BLOCKS):
                for ib in range(NIB):
                    for c in range(CPB):
                        for th in range(2):
                            pr.wait(a, "a", f"fguh_{gup}")
                            if gup >= 2:
                                pr.wait(a, "a", f"fhev_{gup-2}")
                            i = a.activation(silu_g[:, th * 512:(th + 1) * 512],
                                             ps_g[gup % 2][:, :], AF.Silu)
                            pr.inc(i, "act", 1, f"fsilu_{gup}")
                            gup += 1
                            if (DEBUG2 and tb == 0 and ib == 0 and c == 1
                                    and th == 1):
                                pr.wait(a, "a", ("ht", 4))
                                pr.inc(a.dma_start(
                                    out=env["dbg"]["fht1"][:, :],
                                    in_=hT[:, TB:2 * TB]), "st", 16)
                                i = a.dma_start(out=env["dbg"]["fsilu"][:, :],
                                                in_=silu_g[:, :])
                                pr.inc(i, "st", 16, "dbg6_st")
                    if DEBUG2 and tb == 0 and ib == 0:
                        pr.wait(a, "a", "fw_1")
                        pr.inc(a.dma_start(out=env["dbg"]["fwg"][:, :],
                                           in_=wg[1][:, :]), "st", 16)
                        i = a.dma_start(out=env["dbg"]["fwu"][:, :],
                                        in_=wu[1][:, :])
                        pr.inc(i, "st", 16, "dbg5_st")
                        pr.wait(a, "a", ("ht", 2 * CPB))
                        pr.inc(a.dma_start(out=env["dbg"]["flnx"][:, :],
                                           in_=lnx2T[:, :]), "st", 16)
                        i = a.dma_start(out=env["dbg"]["fht"][:, :],
                                        in_=hT[:, :])
                        pr.inc(i, "st", 16, "dbg4_st")

        @block.vector
        def _(v):
            pr.fence_wait(v, "v", "a3_1")
            gup = 0
            dnp = 0
            for tb in range(FFN_BLOCKS):
                for ib in range(NIB):
                    for c in range(CPB):
                        for th in range(2):
                            pr.wait(v, "v", f"fsilu_{gup}")
                            if DEBUG2 and tb == 0 and ib == 0 and c == 2 and th == 0:
                                pr.wait(v, "v", "dbg6_st")
                            if DEBUG2 and tb == 0 and ib == 1 and c == 0 and th == 0:
                                pr.wait(v, "v", "dbg4_st")
                            if tb + ib > 0:
                                pr.wait(v, "v", f"fdn_done_{tb}_{ib-1}" if ib > 0
                                        else f"fdn_done_{tb-1}_{NIB-1}")
                            i = v.scalar_tensor_tensor(
                                hT[:, c * TB + th * 512:c * TB + (th + 1) * 512],
                                ps_u[gup % 2][:, :], 1.0,
                                silu_g[:, th * 512:(th + 1) * 512],
                                ALU.bypass, ALU.mult)
                            pr.inc(i, "ht", 1)
                            pr.mark_here("ht", f"fhev_{gup}")
                            gup += 1
                    for j in range(TPB):
                        gt = tb * TPB + j
                        for half in range(2):
                            pr.wait(v, "v", f"fdn_{dnp}")
                            if ib == 0 and half == 0:
                                pr.wait(v, "v", f"fx2_{gt}")
                                if tb >= 1:
                                    pr.wait(v, "v", f"fst_{tb-1}_{TPB-1}")
                            if ib < 4:
                                scal = sig_buf[:, gt:gt + 1]
                            else:
                                scal = comb_buf[:, gt * E + (ib - 4):
                                                gt * E + (ib - 4) + 1]
                            dst = outb[:, j * H + half * 1024:j * H + (half + 1) * 1024]
                            if ib == 0:
                                src1 = x2t[:, half * 1024:(half + 1) * 1024]
                            else:
                                src1 = dst
                            i = v.scalar_tensor_tensor(
                                dst, ps_d[dnp % 2][:, :], scal, src1,
                                ALU.mult, ALU.add)
                            pr.inc(i, "dve", 1, f"fev_{dnp}")
                            if ib == 0 and half == 1:
                                pr.mark_here("dve", f"fev0_{gt}")
                            if ib == NIB - 1 and half == 1:
                                pr.mark_here("dve", f"fevL_{tb}_{j}")
                            dnp += 1

        @block.sync
        def _(sp):
            tb = FFN_BLOCKS - 1
            for j in range(TPB):
                gt = tb * TPB + j
                pr.wait(sp, "sp", f"fevL_{tb}_{j}")
                i = sp.dma_start(out=out[gt * P:(gt + 1) * P, :],
                                 in_=outb[:, j * H:(j + 1) * H])
                pr.inc(i, "so", 16, f"fst_{tb}_{j}")

        pr.fence_record("ffn")


def make_consts():
    eyef = np.eye(P, dtype=np.float32)
    eyeb = np.eye(P).astype(ml_dtypes.bfloat16)
    zerob = np.zeros((P, P), dtype=ml_dtypes.bfloat16)
    mask = np.triu(np.full((P, P), MASK_NEG, dtype=np.float32), k=1)
    j = np.arange(0, DHEAD, 2, dtype=np.float32) / DHEAD
    freqs = (1.0 / 10000.0 ** j).astype(np.float32)
    t = np.arange(S, dtype=np.float32)
    ang = t[:, None] * freqs[None, :]
    cos = np.cos(ang).astype(np.float32)
    sin = np.sin(ang).astype(np.float32)
    ropec = np.concatenate(
        [cos * np.float32(SCALE), sin * np.float32(SCALE), cos, sin],
        axis=1).astype(ml_dtypes.bfloat16)
    return {
        "c_eyef": eyef, "c_eyeb": eyeb, "c_zerob": zerob, "c_mask": mask,
        "c_rope": np.ascontiguousarray(ropec),
    }


_CACHED = None


def get_program():
    global _CACHED
    if _CACHED is None:
        nc1, pr1 = build_program()
        nc2, pr2 = build_program(marks=pr1.marks)
        assert pr2.missing == 0
        _CACHED = nc2
    return _CACHED


def kernel(L0_attn_x, L0_attn_Wqkv, L0_attn_Wo, L0_ffn_W_router,
           L0_ffn_W_sgate, L0_ffn_Ws_gate_up, L0_ffn_Ws_down,
           L0_ffn_We_gate_up, L0_ffn_We_down):
    nc = get_program()
    consts = make_consts()
    x = np.ascontiguousarray(L0_attn_x, dtype=np.float32).reshape(16 * S, H)
    common = {
        "w_qkv": np.ascontiguousarray(L0_attn_Wqkv),
        "w_o": np.ascontiguousarray(L0_attn_Wo),
        "w_r": np.ascontiguousarray(L0_ffn_W_router),
        "w_sg": np.ascontiguousarray(L0_ffn_W_sgate),
        "w_sgu": np.ascontiguousarray(L0_ffn_Ws_gate_up),
        "w_sd": np.ascontiguousarray(L0_ffn_Ws_down),
        "w_egu": np.ascontiguousarray(L0_ffn_We_gate_up).reshape(E * H, 2 * I_MOE),
        "w_ed": np.ascontiguousarray(L0_ffn_We_down).reshape(E * I_MOE, H),
        **consts,
    }
    in_maps = []
    for c in range(NCORES):
        m = dict(common)
        m["x"] = np.ascontiguousarray(x[c * NB * S:(c + 1) * NB * S, :])
        in_maps.append(m)
    res = run_bass_kernel_spmd(nc, in_maps, core_ids=list(range(NCORES)))
    outs = [np.asarray(r["out"]) for r in res.results]
    return np.concatenate(outs, axis=0).reshape(16, S, H).astype(np.float32)
